# version 24
# speedup vs baseline: 1.2072x; 1.0106x over previous
import contextlib
import numpy as np
import concourse.bass as bass
import concourse.mybir as mybir
from concourse.bass_utils import run_bass_kernel_spmd

F32 = mybir.dt.float32
BF16 = mybir.dt.bfloat16
AF = mybir.ActivationFunctionType
ALU = mybir.AluOpType
AX = mybir.AxisListType
I32 = mybir.dt.int32

D = 2048
KC = 16
S = 2048
TOWN = 1024
TQ = 1280
H = 16
QL = 1536
QLC = 12
KVL = 512
KVLC = 4
FE = 1408
FEC = 11
FS = 5632
NE = 8
NUNIT = 12
SCALE = 128.0 ** -0.5
C1 = 128.0 ** -0.25
EPS = 1e-5
NEG = -30000.0
NSLOT = 8
NW = 4
QB3 = [(0, 512), (512, 512), (1024, 256)]
QB2 = [(0, 512), (512, 512)]
NPHASE = 99
CAP = 512
SPARSE = True

ENGINES = ("pe", "dve", "act", "sp", "pool")


class Buf:
    __slots__ = ("name", "last_w", "readers")

    def __init__(self, name):
        self.name = name
        self.last_w = None
        self.readers = []


class Op:
    __slots__ = ("idx", "eng", "fn", "is_dma", "dma_slot", "dma_round",
                 "milestone", "waits", "drain_before", "tag")

    def __init__(self, idx, eng, fn, is_dma, tag):
        self.idx = idx
        self.eng = eng
        self.fn = fn
        self.is_dma = is_dma
        self.dma_slot = None
        self.dma_round = None
        self.milestone = None
        self.waits = []
        self.drain_before = False
        self.tag = tag


class MK:
    def __init__(self):
        self.ops = []
        self.dma_count = {e: 0 for e in ENGINES}
        self._edges = []
        self.last_op_eng = {e: None for e in ENGINES}
        self.last_dmas = {e: [] for e in ENGINES}
        self._bar_prod = None
        self._bar_unsynced = set()

    def buf(self, name):
        return Buf(name)

    def _new(self, eng, fn, reads, writes, is_dma, tag):
        i = len(self.ops)
        o = Op(i, eng, fn, is_dma, tag)
        self.ops.append(o)
        if self._bar_prod is not None and eng in self._bar_unsynced:
            for p in self._bar_prod:
                if p is not None and p != i:
                    self._edges.append((p, i))
            self._bar_unsynced.discard(eng)
            if not self._bar_unsynced:
                self._bar_prod = None
        for b in reads:
            if b.last_w is not None:
                self._edges.append((b.last_w, i))
        for b in writes:
            if b.last_w is not None:
                self._edges.append((b.last_w, i))
            for r in b.readers:
                self._edges.append((r, i))
        for b in reads:
            b.readers.append(i)
        for b in writes:
            b.last_w = i
            b.readers = []
        self.last_op_eng[eng] = i
        return o

    def op(self, eng, fn, reads=(), writes=(), tag=""):
        return self._new(eng, fn, list(reads), list(writes), False, tag)

    def dma(self, eng, fn, reads=(), writes=(), tag=""):
        o = self._new(eng, fn, list(reads), list(writes), True, tag)
        seq = self.dma_count[eng]
        o.dma_slot = seq % NSLOT
        o.dma_round = seq // NSLOT
        self.dma_count[eng] += 1
        self.last_dmas[eng].append(o.idx)
        if len(self.last_dmas[eng]) > NSLOT:
            self.last_dmas[eng].pop(0)
        return o

    def barrier(self):
        prods = [self.last_op_eng[e] for e in ENGINES]
        for e in ENGINES:
            prods.extend(self.last_dmas[e])
        self._bar_prod = [p for p in prods if p is not None]
        self._bar_unsynced = set(ENGINES)

    def finalize(self, block, qsems, msems):
        ops = self.ops
        red = {}
        for p, c in self._edges:
            if p == c:
                continue
            po = ops[p]
            key = (c, po.eng, po.is_dma, po.dma_slot)
            if red.get(key, -1) < p:
                red[key] = p
        edges = [(p, c) for (c, _e, _d, _s), p in red.items()]
        need_ms = set()
        for p, c in edges:
            po, co = ops[p], ops[c]
            if po.is_dma or po.eng == co.eng:
                continue
            need_ms.add(p)
        ms_count = {e: 0 for e in ENGINES}
        for o in ops:
            if o.idx in need_ms:
                ms_count[o.eng] += 1
                o.milestone = ms_count[o.eng]
        last_drain_cover = {e: -1 for e in ENGINES}
        watermark = {}
        by_consumer = {}
        for p, c in edges:
            by_consumer.setdefault(c, []).append(p)
        for o in ops:
            plist = by_consumer.get(o.idx, ())
            draining = False
            wl = {}
            for p in sorted(set(plist)):
                po = ops[p]
                if po.is_dma:
                    key = ("q", po.eng, po.dma_slot)
                    val = 16 * (po.dma_round + 1)
                    if wl.get(key, 0) < val:
                        wl[key] = val
                elif po.eng == o.eng:
                    if po.eng == "pe":
                        continue
                    if p > last_drain_cover[o.eng]:
                        draining = True
                else:
                    key = ("m", po.eng)
                    if wl.get(key, 0) < po.milestone:
                        wl[key] = po.milestone
            if draining:
                o.drain_before = True
                last_drain_cover[o.eng] = o.idx - 1
            for key, val in wl.items():
                wkey = (o.eng, key)
                if watermark.get(wkey, 0) >= val:
                    continue
                watermark[wkey] = val
                o.waits.append((key, val))
            if o.milestone is not None and o.eng != "pe":
                last_drain_cover[o.eng] = o.idx

        def emit_for(eng, h):
            for o in ops:
                if o.eng != eng:
                    continue
                for key, val in o.waits:
                    sem = qsems[key[1]][key[2]] if key[0] == "q" else msems[key[1]]
                    h.wait_ge(sem, val)
                if o.drain_before:
                    h.drain()
                if o.is_dma and o.dma_round > 0:
                    h.wait_ge(qsems[eng][o.dma_slot], 16 * o.dma_round)
                inst = o.fn(h)
                if o.is_dma:
                    inst.then_inc(qsems[eng][o.dma_slot], 16)
                elif o.milestone is not None:
                    if eng == "pe":
                        inst.then_inc(msems[eng], 1)
                    else:
                        h.drain().then_inc(msems[eng], 1)

        @block.tensor
        def _(te):
            emit_for("pe", te)

        @block.vector
        def _(ve):
            emit_for("dve", ve)

        @block.scalar
        def _(sc):
            emit_for("act", sc)

        @block.sync
        def _(sp):
            emit_for("sp", sp)

        @block.gpsimd
        def _(gp):
            emit_for("pool", gp)

    def stats(self):
        from collections import Counter
        c = Counter(o.eng for o in self.ops)
        return dict(ops=len(self.ops), per_eng=dict(c),
                    drains=sum(1 for o in self.ops if o.drain_before),
                    milestones=sum(1 for o in self.ops if o.milestone),
                    waits=sum(len(o.waits) for o in self.ops))


class B:
    def __init__(self):
        self.nc = bass.Bass("TRN2", target_bir_lowering=False)
        self.mk = MK()
        self.st = contextlib.ExitStack()
        self.wslot = 0
        self.stage_q = []
        self.stg_i = 0
        self.stgsl = []
        self._stage_on = False

    def sb(self, scope, name, shape, dt):
        t = scope.enter_context(self.nc.sbuf_tensor(name, shape, dt))
        b = self.mk.buf(name)
        return t, b

    def inp(self, name, shape, dt=F32):
        ap = self.nc.dram_tensor(name, shape, dt, kind="ExternalInput").ap()
        return ap, self.mk.buf(name)

    def wload(self, src_ap, ncols, src_buf=None, tag="w"):
        i = self.wslot % NW
        self.wslot += 1
        t, b = self.wsl[i]
        reads = [src_buf] if src_buf is not None else []
        self.mk.dma("pool",
                    lambda h, t=t, src=src_ap, n=ncols: h.dma_start(out=t[:, 0:n], in_=src),
                    reads=reads, writes=[b], tag=tag)
        if self._stage_on:
            self.stage_pump(2)
        return t, b

    def wload_c(self, src_ap, ncols, src_buf, tag="w"):
        i = self.wslot % NW
        self.wslot += 1
        t, b = self.wsl[i]
        self.mk.dma("sp",
                    lambda h, t=t, src=src_ap, n=ncols: h.dma_start(out=t[:, 0:n], in_=src),
                    reads=[src_buf], writes=[b], tag=tag)
        return t, b

    def stage_pump(self, n=1):
        for _ in range(n):
            if not self.stage_q:
                return
            src, srcb, dst, db = self.stage_q.pop(0)
            self.mk.dma("pool",
                        lambda h, src=src, dst=dst: h.dma_start(out=dst, in_=src),
                        reads=[srcb], writes=[db], tag="stgl")

    def wload3(self, src_ap, c, m, tag="w"):
        i = self.wslot % NW
        self.wslot += 1
        t, b = self.wsl[i]
        self.mk.dma("pool",
                    lambda h, t=t, src=src_ap, c=c, m=m: h.dma_start(
                        out=t[:, 0:c * m].rearrange("p (c m) -> p c m", c=c),
                        in_=src.rearrange("c p m -> p c m")),
                    writes=[b], tag=tag)
        return t, b

    def mm(self, out_ap, lhsT, rhs, start, stop, reads, writes, tag="mm"):
        self.mk.op("pe", lambda h, o=out_ap, l=lhsT, r=rhs, s=start, e=stop:
                   h.matmul(o, l, r, start=s, stop=e), reads=reads, writes=writes, tag=tag)

    def dve(self, fn, reads, writes, tag=""):
        self.mk.op("dve", fn, reads=reads, writes=writes, tag=tag)

    def dummy_mm(self):
        self.mk.op("pe", lambda h: h.matmul(
            self.PB[0:1, 3 * 512 + 448: 3 * 512 + 512],
            self.onesb[:], self.idb_t[:, 0:64], start=True, stop=True),
            reads=[], writes=[], tag="warm")

    def act(self, fn, reads, writes, tag=""):
        self.mk.op("act", fn, reads=reads, writes=writes, tag=tag)

    def ps_ap(self, bank, lo=0, hi=512, prows=128, p0=0):
        t = self.PS if bank < 4 else self.PB
        c = (bank % 4) * 512
        return t[p0:p0 + prows, c + lo:c + hi]

    def pbuf(self, bank):
        return self.bps[bank] if bank < 4 else self.bpb[bank - 4]

    def _ln_scratch(self, scope, nm):
        self.rows_m = self.sb(scope, nm + "rows_m", [1, 512], F32)
        self.rows_ms = self.sb(scope, nm + "rows_ms", [1, 512], F32)
        self.rows_r = self.sb(scope, nm + "rows_r", [1, 512], F32)
        self.sqb = [self.sb(scope, nm + f"sqb{i}", [128, 512], BF16) for i in range(2)]
        self.xcast = [self.sb(scope, nm + f"xcast{i}", [128, 512], BF16) for i in range(2)]
        self.tmpc = self.sb(scope, nm + "tmpc", [128, 512], F32)

    def emit_ln_block(self, tn, src_chunk, xn_out, tag):
        mk = self.mk
        for c in range(KC):
            sap, sbuf, isbf = src_chunk(c)
            sq, bsq = self.sqb[c % 2]
            self.act(lambda h, o=sq[:, 0:tn], i=sap: h.activation(out=o, in_=i, func=AF.Square),
                     reads=[sbuf], writes=[bsq], tag=tag + ".sq")
            if isbf:
                mrhs, mbuf = sap, sbuf
            else:
                xb, bxb = self.xcast[c % 2]
                self.dve(lambda h, o=xb[:, 0:tn], i=sap: h.tensor_copy(out=o, in_=i),
                         reads=[sbuf], writes=[bxb], tag=tag + ".cast")
                mrhs, mbuf = xb[:, 0:tn], bxb
            self.mm(self.ps_ap(6, 0, tn, prows=1), self.onesb[:], mrhs,
                    c == 0, c == KC - 1, [self.b_onesb, mbuf], [self.pbuf(6)], tag + ".mmm")
            self.mm(self.ps_ap(7, 0, tn, prows=1), self.onesb[:], sq[:, 0:tn],
                    c == 0, c == KC - 1, [self.b_onesb, bsq], [self.pbuf(7)], tag + ".mms")
        rm, b_rm = self.rows_m
        rs, b_rs = self.rows_ms
        rr, b_rr = self.rows_r
        self.act(lambda h, o=rm[0:1, 0:tn], i=self.ps_ap(6, 0, tn, prows=1):
                 h.mul(out=o, in_=i, mul=1.0 / D),
                 reads=[self.pbuf(6)], writes=[b_rm], tag=tag + ".rm")
        self.act(lambda h, o=rs[0:1, 0:tn], i=self.ps_ap(7, 0, tn, prows=1):
                 h.mul(out=o, in_=i, mul=1.0 / D),
                 reads=[self.pbuf(7)], writes=[b_rs], tag=tag + ".rs")
        self.dve(lambda h, o=rr[0:1, 0:tn], a=rm[0:1, 0:tn]: h.tensor_mul(out=o, in0=a, in1=a),
                 reads=[b_rm], writes=[b_rr], tag=tag + ".m2")
        self.dve(lambda h, o=rs[0:1, 0:tn], a=rs[0:1, 0:tn], b=rr[0:1, 0:tn]:
                 h.tensor_sub(out=o, in0=a, in1=b),
                 reads=[b_rs, b_rr], writes=[b_rs], tag=tag + ".var")
        self.act(lambda h, o=rs[0:1, 0:tn], i=rs[0:1, 0:tn], bb=self.epsb[0:1, 0:1]:
                 h.activation(out=o, in_=i, func=AF.Sqrt, bias=bb),
                 reads=[b_rs, self.b_epsb], writes=[b_rs], tag=tag + ".sd")
        self.dve(lambda h, o=rr[0:1, 0:tn], i=rs[0:1, 0:tn]: h.reciprocal(out=o, in_=i),
                 reads=[b_rs], writes=[b_rr], tag=tag + ".r")
        self.mm(self.ps_ap(4, 0, tn), self.onesf[:], rm[0:1, 0:tn], True, True,
                [self.b_onesf, b_rm], [self.pbuf(4)], tag + ".bm")
        self.mm(self.ps_ap(5, 0, tn), self.onesf[:], rr[0:1, 0:tn], True, True,
                [self.b_onesf, b_rr], [self.pbuf(5)], tag + ".br")
        for c in range(KC):
            sap, sbuf, _isbf = src_chunk(c)
            tc, btc = self.tmpc
            self.dve(lambda h, o=tc[:, 0:tn], a=sap, b=self.ps_ap(4, 0, tn):
                     h.tensor_sub(out=o, in0=a, in1=b),
                     reads=[sbuf, self.pbuf(4)], writes=[btc], tag=tag + ".sub")
            oap, obuf = xn_out(c)
            self.dve(lambda h, o=oap, a=tc[:, 0:tn], b=self.ps_ap(5, 0, tn):
                     h.tensor_mul(out=o, in0=a, in1=b),
                     reads=[btc, self.pbuf(5)], writes=[obuf], tag=tag + ".mul")

    def build(self):
        nc, mk = self.nc, self.mk
        en = self.st.enter_context

        i_xq, b_xq = self.inp("xq", [D, TQ])
        i_xkv, b_xkv = self.inp("xkv", [D, S])
        i_causal, b_causal = self.inp("causal", [128, 128])
        i_dmA, b_dmA = self.inp("dmA", [128, 128])
        i_id, b_id = self.inp("ident", [128, 128])
        i_hflag, b_hflag = self.inp("hflag", [1, 128])
        i_cosq, b_cosq = self.inp("cosq", [64, TQ])
        i_sinq, b_sinq = self.inp("sinq", [64, TQ])
        i_cosk, b_cosk = self.inp("cosk", [64, S])
        i_sink, b_sink = self.inp("sink", [64, S])
        i_bmask, b_bmask = self.inp("bmask", [128, 8 * 384])
        i_wqa, b_wqa = self.inp("wqa", [QLC, 128, D])
        i_wqb, b_wqb = self.inp("wqb", [H, 128, QL])
        i_wkva_c, b_wkva_c = self.inp("wkva_c", [KVLC, 128, D])
        i_wkva_p, b_wkva_p = self.inp("wkva_p", [128, KC * 64])
        i_wkvb_n, b_wkvb_n = self.inp("wkvb_n", [8, 128, KVL])
        i_wkvb_v, b_wkvb_v = self.inp("wkvb_v", [8, KVLC, 128, 256])
        i_woa, b_woa = self.inp("woa", [KC, 128, D])
        i_wqbb, b_wqbb = self.inp("wqbb", [H, 128, D])
        i_wkvb2, b_wkvb2 = self.inp("wkvb2", [KVLC, 128, D])
        i_wkb, b_wkb = self.inp("wkb", [H, 128, KVL])
        i_wvb, b_wvb = self.inp("wvb", [4, KVLC, 128, 512])
        i_wob, b_wob = self.inp("wob", [KC, 128, D])
        i_wrsg, b_wrsg = self.inp("wrsg", [128, KC * 9])
        i_wg, b_wg = self.inp("wg", [NE, FEC, 128, D])
        i_wu, b_wu = self.inp("wu", [NE, FEC, 128, D])
        i_wd, b_wd = self.inp("wd", [NE, KC, 128, FE])
        i_sg, b_sg = self.inp("sgw", [FEC * 4, 128, D])
        i_su, b_su = self.inp("suw", [FEC * 4, 128, D])
        i_sd, b_sd = self.inp("sdw", [KC, 4, 128, FE])
        i_triu, b_triu = self.inp("triu", [128, 128])
        i_tri8, b_tri8 = self.inp("tri8", [80, 80])
        i_tokid, b_tokid = self.inp("tokid", [128, 10], I32)
        d_xnd = self.nc.dram_tensor("xnd", [TQ, D], BF16).ap()
        b_xnd = self.mk.buf("xnd")
        d_cdram = self.nc.dram_tensor("cdram", [TQ, 8], F32).ap()
        b_cdram = self.mk.buf("cdram")
        d_idxd = self.nc.dram_tensor("idxd", [NE * CAP, 1], I32).ap()
        b_idxd = [self.mk.buf(f"idxd{e}") for e in range(NE)]
        d_ye = self.nc.dram_tensor("yexp", [NE * CAP, D], BF16).ap()
        b_ye = self.mk.buf("yexp")
        self.sparse_res = dict(d_xnd=d_xnd, b_xnd=b_xnd, d_cdram=d_cdram,
                               b_cdram=b_cdram, d_idxd=d_idxd, b_idxd=b_idxd,
                               d_ye=d_ye, b_ye=b_ye)
        d_csg = self.nc.dram_tensor("csg", [FEC * 4, 128, D], BF16).ap()
        d_csu = self.nc.dram_tensor("csu", [FEC * 4, 128, D], BF16).ap()
        d_csd = self.nc.dram_tensor("csd", [KC, 4, 128, FE], BF16).ap()
        b_csg = [mk.buf(f"csg{k}") for k in range(FEC * 4)]
        b_csu = [mk.buf(f"csu{k}") for k in range(FEC * 4)]
        b_csd = [mk.buf(f"csd{k}") for k in range(KC * 4)]
        for k in range(FEC * 4):
            self.stage_q.append((i_sg[k], b_sg, d_csg[k], b_csg[k]))
            self.stage_q.append((i_su[k], b_su, d_csu[k], b_csu[k]))

        d_cwg = self.nc.dram_tensor("cwg", [NE, FEC, 128, D], BF16).ap()
        d_cwu = self.nc.dram_tensor("cwu", [NE, FEC, 128, D], BF16).ap()
        d_cwd = self.nc.dram_tensor("cwd", [NE, KC, 128, FE], BF16).ap()
        b_cwg = [mk.buf(f"cwg{k}") for k in range(NE * FEC)]
        b_cwu = [mk.buf(f"cwu{k}") for k in range(NE * FEC)]
        b_cwd = [mk.buf(f"cwd{k}") for k in range(NE * KC)]
        for o_ in range(KC):
            for su in range(4):
                self.stage_q.append((i_sd[o_, su], b_sd,
                                     d_csd[o_, su], b_csd[o_ * 4 + su]))
        self.stage_q2 = []
        for e in range(NE):
            for o_ in range(KC):
                self.stage_q2.append((i_wd[e, o_], b_wd, d_cwd[e, o_], b_cwd[e * KC + o_]))
        self.wcache = dict(d_csg=d_csg, b_csg=b_csg, d_csu=d_csu, b_csu=b_csu,
                           d_csd=d_csd, b_csd=b_csd,
                           d_cwg=d_cwg, b_cwg=b_cwg, d_cwu=d_cwu, b_cwu=b_cwu,
                           d_cwd=d_cwd, b_cwd=b_cwd)
        o_y = self.nc.dram_tensor("y", [D, TOWN], F32, kind="ExternalOutput").ap()
        b_y = mk.buf("y")

        qs = {e: [en(nc.semaphore(f"q_{e}{i}")) for i in range(NSLOT)] for e in ENGINES}
        self.rb_cap = en(nc.gpsimd.register("rb_cap"))
        self.rb_tq = en(nc.gpsimd.register("rb_tq"))
        self.rb_own = en(nc.gpsimd.register("rb_own"))
        self.rb_ye = en(nc.gpsimd.register("rb_ye"))
        ms = {e: en(nc.semaphore(f"m_{e}")) for e in ENGINES}
        block = en(nc.Block())

        S_, b_S = self.sb(self.st, "stream", [128, KC * TQ], F32)
        self.wsl = [self.sb(self.st, f"wsl{i}", [128, 2048], BF16) for i in range(NW)]
        causal, b_causal_s = self.sb(self.st, "causal_s", [128, 128], F32)
        dmA, b_dmA_s = self.sb(self.st, "dmA_s", [128, 128], F32)
        idf, b_idf = self.sb(self.st, "idf", [128, 128], F32)
        idb, b_idb = self.sb(self.st, "idb", [128, 128], BF16)
        self.idb_t = idb
        self.b_idb = b_idb
        onesb, self.b_onesb = self.sb(self.st, "onesb", [128, 1], BF16)
        self.onesb = onesb
        onesf, self.b_onesf = self.sb(self.st, "onesf", [1, 128], F32)
        self.onesf = onesf
        id1, self.b_id1 = self.sb(self.st, "id1", [1, 1], F32)
        self.id1 = id1
        epsb, self.b_epsb = self.sb(self.st, "epsb", [1, 1], F32)
        self.epsb = epsb
        id1b, self.b_id1b = self.sb(self.st, "id1b", [1, 1], BF16)
        self.id1b = id1b
        trib, self.b_trib = self.sb(self.st, "trib", [128, 128], BF16)
        self.trib = trib
        tri8b, self.b_tri8b = self.sb(self.st, "tri8b", [80, 80], BF16)
        self.tri8b = tri8b
        tokids, self.b_tokids = self.sb(self.st, "tokids", [128, 10], I32)
        self.tokids = tokids
        minusb, b_minusb = self.sb(self.st, "minusb", [1, 512], BF16)
        hflagb, b_hflagb = self.sb(self.st, "hflagb", [1, 128], BF16)
        colsA, b_colsA = self.sb(self.st, "colsA", [128, 16], F32)
        lrow, b_lrow = self.sb(self.st, "lrow", [1, 128], F32)

        self.PS = en(nc.psum_tensor("PS", [128, 2048], F32))
        self.PB = en(nc.psum_tensor("PB", [128, 2048], F32))
        self.bps = [mk.buf(f"ps{i}") for i in range(4)]
        self.bpb = [mk.buf(f"pb{i}") for i in range(4)]

        mk.dma("sp", lambda h: h.dma_start(
            out=S_[:].rearrange("p (c t) -> p c t", c=KC),
            in_=i_xq.rearrange("(c p) t -> p c t", p=128)),
            reads=[b_xq], writes=[b_S], tag="ld_stream")
        mk.dma("sp", lambda h: h.dma_start(out=causal[:], in_=i_causal),
               reads=[b_causal], writes=[b_causal_s], tag="ld_causal")
        mk.dma("sp", lambda h: h.dma_start(out=dmA[:], in_=i_dmA),
               reads=[b_dmA], writes=[b_dmA_s], tag="ld_dmA")
        mk.dma("sp", lambda h: h.dma_start(out=idf[:], in_=i_id),
               reads=[b_id], writes=[b_idf], tag="ld_id")
        mk.dma("pool", lambda h: h.dma_start(out=hflagb[:], in_=i_hflag),
               reads=[b_hflag], writes=[b_hflagb], tag="ld_hflag")
        mk.op("pool", lambda h: h.reg_mov(self.rb_cap, CAP - 1), tag="rb1")
        mk.op("pool", lambda h: h.reg_mov(self.rb_tq, TQ - 1), tag="rb2")
        mk.op("pool", lambda h: h.reg_mov(self.rb_own, TOWN - 1), tag="rb3")
        mk.op("pool", lambda h: h.reg_mov(self.rb_ye, NE * CAP - 1), tag="rb4")
        mk.dma("pool", lambda h, o=trib: h.dma_start(out=o[:], in_=i_triu),
               reads=[b_triu], writes=[self.b_trib], tag="ld_triu")
        mk.dma("pool", lambda h, o=tri8b: h.dma_start(out=o[:], in_=i_tri8),
               reads=[b_tri8], writes=[self.b_tri8b], tag="ld_tri8")
        mk.dma("sp", lambda h, o=tokids: h.dma_start(out=o[:], in_=i_tokid),
               reads=[b_tokid], writes=[self.b_tokids], tag="ld_tokid")
        self.dve(lambda h: h.memset(onesb[:], 1.0), [], [self.b_onesb], "c1")
        self.dve(lambda h, o=idb[:], i=idf[:]: h.tensor_copy(out=o, in_=i),
                 reads=[b_idf], writes=[b_idb], tag="c1b")
        self.dve(lambda h: h.memset(onesf[:], 1.0), [], [self.b_onesf], "c2")
        self.dve(lambda h: h.memset(id1[:], 1.0), [], [self.b_id1], "c2b")
        self.dve(lambda h: h.memset(epsb[:], EPS), [], [self.b_epsb], "c2c")
        self.dve(lambda h: h.memset(id1b[:], 1.0), [], [self.b_id1b], "c2d")
        self.dve(lambda h: h.memset(minusb[:], NEG), [], [b_minusb], "c3")

        sc_kv = contextlib.ExitStack()
        ckv, b_ckv = self.sb(sc_kv, "ckv", [128, KVLC * S], BF16)
        kpe, b_kpe = self.sb(sc_kv, "kpe", [64, S], BF16)

        sc1 = contextlib.ExitStack()
        self._ln_scratch(sc1, "p1b")
        X0 = [self.sb(sc1, f"X0_{i}", [128, KC * 512], BF16) for i in range(2)]
        xnb, b_xnb = self.sb(sc1, "xnb", [128, KC * 512], BF16)
        kps, b_kps = self.sb(sc1, "kpslab", [128, KC * 64], BF16)
        ckt = [self.sb(sc1, f"ckt{i}", [64, 512], F32) for i in range(2)]
        tu = [self.sb(sc1, f"tu{i}", [64, 512], F32) for i in range(3)]

        mk.dma("pool", lambda h: h.dma_start(out=kps[:], in_=i_wkva_p),
               reads=[b_wkva_p], writes=[b_kps], tag="ld_kps")
        ckv_slabs = [self.wload(i_wkva_c[oc], D, b_wkva_c, tag=f"wkvac{oc}")
                     for oc in range(KVLC)]

        for sb_i in range(4):
            xt, b_xt = X0[sb_i % 2]
            cc, b_cc = ckt[0]
            ss, b_ss = ckt[1]
            mk.dma("pool", lambda h, o=xt, s=sb_i: h.dma_start(
                out=o[:].rearrange("p (c t) -> p c t", c=KC),
                in_=i_xkv[:, s * 512:(s + 1) * 512].rearrange("(c p) t -> p c t", p=128)),
                reads=[b_xkv], writes=[b_xt], tag=f"ld_x{sb_i}")
            mk.dma("sp", lambda h, o=cc, s=sb_i: h.dma_start(out=o[:], in_=i_cosk[:, s * 512:(s + 1) * 512]),
                   reads=[b_cosk], writes=[b_cc], tag="ld_ck")
            mk.dma("sp", lambda h, o=ss, s=sb_i: h.dma_start(out=o[:], in_=i_sink[:, s * 512:(s + 1) * 512]),
                   reads=[b_sink], writes=[b_ss], tag="ld_sk")

            self.emit_ln_block(
                512,
                lambda c, xt=xt, b_xt=b_xt: (xt[:, c * 512:(c + 1) * 512], b_xt, True),
                lambda c: (xnb[:, c * 512:(c + 1) * 512], b_xnb),
                f"ln1.{sb_i}")

            for oc in range(KVLC):
                wt, bw = ckv_slabs[oc]
                bank = oc % 2
                for kcc in range(KC):
                    self.mm(self.ps_ap(bank), wt[:, kcc * 128:(kcc + 1) * 128],
                            xnb[:, kcc * 512:(kcc + 1) * 512],
                            kcc == 0, kcc == KC - 1, [bw, b_xnb], [self.pbuf(bank)], "ckv")
                self.act(lambda h, o=ckv[:, oc * S + sb_i * 512: oc * S + (sb_i + 1) * 512],
                         i=self.ps_ap(bank): h.copy(out=o, in_=i),
                         reads=[self.pbuf(bank)], writes=[b_ckv], tag="ckv.ev")
            for kcc in range(KC):
                self.mm(self.ps_ap(2, 0, 512, prows=64), kps[:, kcc * 64:(kcc + 1) * 64],
                        xnb[:, kcc * 512:(kcc + 1) * 512],
                        kcc == 0, kcc == KC - 1, [b_kps, b_xnb], [self.pbuf(2)], "kpe")
            t0, bt0 = tu[0]
            t1, bt1 = tu[1]
            t2, bt2 = tu[2]
            self.dve(lambda h, o=t0[:], a=self.ps_ap(2, 0, 512, prows=64), b=cc[:]:
                     h.tensor_mul(out=o, in0=a, in1=b),
                     reads=[self.pbuf(2), b_cc], writes=[bt0], tag="rk1")
            self.dve(lambda h, o=t1[:], a=self.ps_ap(2, 0, 512, prows=64), b=ss[:]:
                     h.tensor_mul(out=o, in0=a, in1=b),
                     reads=[self.pbuf(2), b_ss], writes=[bt1], tag="rk2")
            self.dve(lambda h, o=t2[0:32, :], i=t1[32:64, :]: h.tensor_copy(out=o, in_=i),
                     reads=[bt1], writes=[bt2], tag="rk2a")
            self.dve(lambda h, o=t2[32:64, :], i=t1[0:32, :]: h.tensor_copy(out=o, in_=i),
                     reads=[bt1], writes=[bt2], tag="rk2b")
            self.dve(lambda h, o=kpe[0:32, sb_i * 512:(sb_i + 1) * 512], a=t0[0:32, :], b=t2[0:32, :]:
                     h.tensor_sub(out=o, in0=a, in1=b),
                     reads=[bt0, bt2], writes=[b_kpe], tag="rk3")
            self.dve(lambda h, o=kpe[32:64, sb_i * 512:(sb_i + 1) * 512], a=t0[32:64, :], b=t2[32:64, :]:
                     h.tensor_add(out=o, in0=a, in1=b),
                     reads=[bt0, bt2], writes=[b_kpe], tag="rk4")
        sc1.close()
        mk.barrier()
        if NPHASE < 2:
            self._stage_on = False
            sc_kv.close()
            return self.finish(block, qs, ms, S_, b_S, o_y, b_y)

        sc_q = contextlib.ExitStack()
        qlat, b_qlat = self.sb(sc_q, "qlat", [128, QLC * TQ], BF16)
        cosq, b_cosq_s = self.sb(sc_q, "cosq_s", [64, TQ], F32)
        sinq, b_sinq_s = self.sb(sc_q, "sinq_s", [64, TQ], F32)
        mk.dma("sp", lambda h: h.dma_start(out=cosq[:], in_=i_cosq),
               reads=[b_cosq], writes=[b_cosq_s], tag="ld_cosq")
        mk.dma("sp", lambda h: h.dma_start(out=sinq[:], in_=i_sinq),
               reads=[b_sinq], writes=[b_sinq_s], tag="ld_sinq")

        sc2 = contextlib.ExitStack()
        self._ln_scratch(sc2, "p1a")
        xnq, b_xnq = self.sb(sc2, "xnq", [128, KC * 512], BF16)
        for (t0c, tn) in QB3:
            self.emit_ln_block(
                tn,
                lambda c, t0c=t0c, tn=tn: (S_[:, c * TQ + t0c: c * TQ + t0c + tn], b_S, False),
                lambda c, tn=tn: (xnq[:, c * 512: c * 512 + tn], b_xnq),
                f"ln1q.{t0c}")
            for oc in range(QLC):
                wt, bw = self.wload(i_wqa[oc], D, b_wqa, tag=f"wqa{oc}")
                bank = oc % 2
                for kcc in range(KC):
                    self.mm(self.ps_ap(bank, 0, tn), wt[:, kcc * 128:(kcc + 1) * 128],
                            xnq[:, kcc * 512: kcc * 512 + tn],
                            kcc == 0, kcc == KC - 1, [bw, b_xnq], [self.pbuf(bank)], "qlat")
                self.act(lambda h, o=qlat[:, oc * TQ + t0c: oc * TQ + t0c + tn],
                         i=self.ps_ap(bank, 0, tn): h.copy(out=o, in_=i),
                         reads=[self.pbuf(bank)], writes=[b_qlat], tag="qlat.ev")
        sc2.close()
        mk.barrier()
        if NPHASE < 3:
            sc_q.close()
            self._stage_on = False
            sc_kv.close()
            return self.finish(block, qs, ms, S_, b_S, o_y, b_y)

        self._stage_on = True
        sc3 = contextlib.ExitStack()
        kf2, b_kf2 = self.sb(sc3, "kf2", [128, 2 * S], BF16)
        v2, b_v2 = self.sb(sc3, "v2", [128, KC * 256], BF16)
        qh2, b_qh2 = self.sb(sc3, "qh2", [128, 2 * TQ], BF16)
        aall, b_aall = self.sb(sc3, "aall", [128, 4 * TQ], BF16)
        probs2 = [self.sb(sc3, f"probs{i}", [128, 512], BF16) for i in range(2)]
        avsb, b_avsb = self.sb(sc3, "avsb", [128, 128], BF16)
        pT2 = [self.sb(sc3, f"pT{i}", [128, 512], BF16) for i in range(2)]
        qrot, b_qrot = self.sb(sc3, "qrot", [64, 512], BF16)
        tua, b_tua = self.sb(sc3, "tua", [64, 512], F32)
        tub, b_tub = self.sb(sc3, "tub", [64, 512], F32)
        tuc, b_tuc = self.sb(sc3, "tuc", [64, 512], F32)
        sq2, b_sq2 = self.sb(sc3, "sq2", [128, 1024], BF16)
        nbrow, b_nbrow = self.sb(sc3, "nbrow", [1, TQ], BF16)
        krow, b_krow = self.sb(sc3, "krow", [1, 8], F32)

        for pz in range(4):
            for gg in range(2):
                g = pz * 2 + gg
                wt, bw = self.wload(i_wkvb_n[g], KVL, b_wkvb_n, tag=f"wkn{g}")
                for kb in range(4):
                    bank = 5 + kb % 2
                    for kcc in range(KVLC):
                        self.mm(self.ps_ap(bank), wt[:, kcc * 128:(kcc + 1) * 128],
                                ckv[:, kcc * S + kb * 512: kcc * S + (kb + 1) * 512],
                                kcc == 0, kcc == KVLC - 1, [bw, b_ckv], [self.pbuf(bank)], "kn")
                    self.act(lambda h, o=kf2[0:64, kb * 512:(kb + 1) * 512],
                             i=self.ps_ap(bank, prows=64): h.mul(out=o, in_=i, mul=C1),
                             reads=[self.pbuf(bank)], writes=[b_kf2], tag="kn.ev0")
                    self.act(lambda h, o=kf2[0:64, S + kb * 512: S + (kb + 1) * 512],
                             i=self.ps_ap(bank, prows=64, p0=64):
                             h.mul(out=o, in_=i, mul=C1),
                             reads=[self.pbuf(bank)], writes=[b_kf2], tag="kn.ev1")
                for hh in range(2):
                    self.dve(lambda h, o=kf2[64:128, hh * S: hh * S + S], i=kpe[0:64, :]:
                             h.tensor_copy(out=o, in_=i),
                             reads=[b_kpe], writes=[b_kf2], tag="kpe.cp")
                wt, bw = self.wload3(i_wkvb_v[g], KVLC, 256, tag=f"wv{g}")
                for tc in range(KC):
                    bank = 5 + tc % 2
                    for kcc in range(KVLC):
                        self.mm(self.ps_ap(bank, 0, 256), ckv[:, kcc * S + tc * 128: kcc * S + (tc + 1) * 128],
                                wt[:, kcc * 256:(kcc + 1) * 256],
                                kcc == 0, kcc == KVLC - 1, [bw, b_ckv], [self.pbuf(bank)], "v")
                    self.dve(lambda h, o=v2[:, tc * 256:(tc + 1) * 256], i=self.ps_ap(bank, 0, 256):
                             h.tensor_copy(out=o, in_=i),
                             reads=[self.pbuf(bank)], writes=[b_v2], tag="v.ev")
                for hh in range(2):
                    head = 2 * g + hh
                    wt, bw = self.wload(i_wqb[head], QL, b_wqb, tag=f"wqb{head}")
                    for qbi, (t0c, tn) in enumerate(QB3):
                        bank = 5 + qbi % 2
                        for lc in range(QLC):
                            self.mm(self.ps_ap(bank, 0, tn), wt[:, lc * 128:(lc + 1) * 128],
                                    qlat[:, lc * TQ + t0c: lc * TQ + t0c + tn],
                                    lc == 0, lc == QLC - 1, [bw, b_qlat], [self.pbuf(bank)], "qh")
                        self.act(lambda h, o=qh2[0:64, hh * TQ + t0c: hh * TQ + t0c + tn],
                                 i=self.ps_ap(bank, 0, tn, prows=64): h.mul(out=o, in_=i, mul=C1),
                                 reads=[self.pbuf(bank)], writes=[b_qh2], tag="qh.nope")
                        self.dve(lambda h, o=tua[:, 0:tn], a=self.ps_ap(bank, 0, tn, prows=64, p0=64),
                                 b=cosq[:, t0c:t0c + tn]:
                                 h.tensor_mul(out=o, in0=a, in1=b),
                                 reads=[self.pbuf(bank), b_cosq_s], writes=[b_tua], tag="qr1")
                        self.dve(lambda h, o=tub[:, 0:tn], a=self.ps_ap(bank, 0, tn, prows=64, p0=64),
                                 b=sinq[:, t0c:t0c + tn]:
                                 h.tensor_mul(out=o, in0=a, in1=b),
                                 reads=[self.pbuf(bank), b_sinq_s], writes=[b_tub], tag="qr2")
                        self.dve(lambda h, o=tuc[0:32, 0:tn], i=tub[32:64, 0:tn]:
                                 h.tensor_copy(out=o, in_=i),
                                 reads=[b_tub], writes=[b_tuc], tag="qr2a")
                        self.dve(lambda h, o=tuc[32:64, 0:tn], i=tub[0:32, 0:tn]:
                                 h.tensor_copy(out=o, in_=i),
                                 reads=[b_tub], writes=[b_tuc], tag="qr2b")
                        self.dve(lambda h, o=qrot[0:32, 0:tn], a=tua[0:32, 0:tn], b=tuc[0:32, 0:tn]:
                                 h.tensor_sub(out=o, in0=a, in1=b),
                                 reads=[b_tua, b_tuc], writes=[b_qrot], tag="qr3")
                        self.dve(lambda h, o=qrot[32:64, 0:tn], a=tua[32:64, 0:tn], b=tuc[32:64, 0:tn]:
                                 h.tensor_add(out=o, in0=a, in1=b),
                                 reads=[b_tua, b_tuc], writes=[b_qrot], tag="qr4")
                        self.dve(lambda h, o=qh2[64:128, hh * TQ + t0c: hh * TQ + t0c + tn],
                                 i=qrot[0:64, 0:tn]: h.tensor_copy(out=o, in_=i),
                                 reads=[b_qrot], writes=[b_qh2], tag="qr5")
                for hh in range(2):
                    head = 2 * g + hh
                    slot = head % 4
                    for kb in range(4):
                        bank = 5 + kb % 2
                        self.act(lambda h, o=sq2[:, (kb % 2) * 512:(kb % 2) * 512 + 512],
                                 i=kf2[:, hh * S + kb * 512: hh * S + (kb + 1) * 512]:
                                 h.activation(out=o, in_=i, func=AF.Square),
                                 reads=[b_kf2], writes=[b_sq2], tag="nb.ksq")
                        self.mm(self.ps_ap(bank, 0, 512, prows=1), self.onesb[:],
                                sq2[:, (kb % 2) * 512:(kb % 2) * 512 + 512],
                                True, True, [self.b_onesb, b_sq2], [self.pbuf(bank)], "nb.kmm")
                        self.dve(lambda h, o=krow[0:1, kb:kb + 1], i=self.ps_ap(bank, 0, 512, prows=1):
                                 h.reduce_max(o, i, axis=AX.X),
                                 reads=[self.pbuf(bank)], writes=[b_krow], tag="nb.kmx")
                    self.dve(lambda h, o=krow[0:1, 4:5], i=krow[0:1, 0:4]:
                             h.reduce_max(o, i, axis=AX.X),
                             reads=[b_krow], writes=[b_krow], tag="nb.kmx2")
                    for qbi, (t0c, tn) in enumerate(QB3):
                        bank = 5 + qbi % 2
                        self.act(lambda h, o=sq2[:, (qbi % 2) * 512:(qbi % 2) * 512 + tn],
                                 i=qh2[:, hh * TQ + t0c: hh * TQ + t0c + tn]:
                                 h.activation(out=o, in_=i, func=AF.Square),
                                 reads=[b_qh2], writes=[b_sq2], tag="nb.qsq")
                        self.mm(self.ps_ap(bank, 0, tn, prows=1), self.onesb[:],
                                sq2[:, (qbi % 2) * 512:(qbi % 2) * 512 + tn],
                                True, True, [self.b_onesb, b_sq2], [self.pbuf(bank)], "nb.qmm")
                        self.dve(lambda h, o=nbrow[0:1, t0c:t0c + tn], i=self.ps_ap(bank, 0, tn, prows=1),
                                 sc=krow[0:1, 4:5]:
                                 h.tensor_scalar(out=o, in0=i, scalar1=sc, scalar2=0.5,
                                                 op0=ALU.add, op1=ALU.mult),
                                 reads=[self.pbuf(bank), b_krow], writes=[b_nbrow], tag="nb.q2")
                    for qt in range(10):
                        bank = 5 + qt % 2
                        self.mk.op("pe", lambda h, o=self.ps_ap(bank).bitcast(BF16)[:, 0:1],
                                   i=nbrow[0:1, qt * 128:(qt + 1) * 128],
                                   dd=self.id1b[:]: h.transpose(o, i, dd),
                                   reads=[b_nbrow, self.b_id1b], writes=[self.pbuf(bank)], tag="nb.tr")
                        self.dve(lambda h, o=colsA[:, qt:qt + 1], i=self.ps_ap(bank).bitcast(BF16)[:, 0:1]:
                                 h.tensor_scalar(out=o, in0=i, scalar1=-1.0, scalar2=None, op0=ALU.mult),
                                 reads=[self.pbuf(bank)], writes=[b_colsA], tag="nb.ev")
                    for qt in range(10):
                        if qt < 8:
                            Kj = (qt + 9) * 128
                        elif qt == 8:
                            Kj = 128
                        else:
                            Kj = 1024
                        qsl = qh2[:, hh * TQ + qt * 128: hh * TQ + (qt + 1) * 128]
                        npc = (Kj + 511) // 512
                        dcols = []
                        if qt < 8:
                            dcols = [(qt * 128, dmA[:], b_dmA_s), (Kj - 128, causal[:], b_causal_s)]
                        elif qt == 8:
                            dcols = [(0, causal[:], b_causal_s)]
                        else:
                            dcols = [(896, causal[:], b_causal_s)]
                        for bkk in range(npc):
                            lo = bkk * 512
                            hi = min(Kj, lo + 512)
                            bank_ops = [("sc", lo, hi)]
                            if qt < 8:
                                mlo = max(lo, (qt + 1) * 128)
                                if mlo < hi:
                                    bank_ops.append(("cm", mlo, hi))
                            first = True
                            for ii, (kind, a0, a1) in enumerate(bank_ops):
                                last = ii == len(bank_ops) - 1
                                if kind == "sc":
                                    self.mm(self.ps_ap(bkk, a0 - lo, a1 - lo),
                                            qsl, kf2[:, hh * S + a0: hh * S + a1],
                                            first, last, [b_qh2, b_kf2], [self.pbuf(bkk)], "sc")
                                else:
                                    self.mm(self.ps_ap(bkk, a0 - lo, a1 - lo, prows=128),
                                            hflagb[:], minusb[0:1, 0:a1 - a0],
                                            first, last, [b_hflagb, b_minusb], [self.pbuf(bkk)], "cm")
                                first = False
                            for (col, msk, mbuf) in dcols:
                                if lo <= col < hi:
                                    off = col - lo
                                    self.dve(lambda h, o=self.ps_ap(bkk, off, off + 128), m=msk:
                                             h.tensor_tensor(out=o, in0=o, in1=m, op=ALU.add),
                                             reads=[self.pbuf(bkk), mbuf], writes=[self.pbuf(bkk)], tag="dm")
                        nkt = Kj // 128
                        for bkk in range(npc):
                            probs, b_probs = probs2[bkk % 2]
                            pT, b_pT = pT2[bkk % 2]
                            hi = min(Kj, (bkk + 1) * 512) - bkk * 512
                            self.act(lambda h, o=probs[:, 0:hi],
                                     i=self.ps_ap(bkk, 0, hi), bb=colsA[:, qt:qt + 1],
                                     ac=colsA[:, 10 + bkk:11 + bkk]:
                                     h.activation(out=o, in_=i, func=AF.Exp, bias=bb, accum_out=ac),
                                     reads=[self.pbuf(bkk), b_colsA], writes=[b_probs, b_colsA], tag="exp")
                            npk = hi // 128
                            pbank = 4 if bkk % 2 == 0 else 6
                            for j in range(npk):
                                self.mk.op("pe", lambda h, o=self.ps_ap(pbank).bitcast(BF16)[:, j * 128:(j + 1) * 128],
                                           i=probs[:, j * 128:(j + 1) * 128], dd=idb[:]:
                                           h.transpose(o, i, dd),
                                           reads=[b_probs, b_idb], writes=[self.pbuf(pbank)], tag="ptr")
                            self.dve(lambda h, o=pT[:, 0:npk * 128],
                                     i=self.ps_ap(pbank).bitcast(BF16)[:, 0:npk * 128]:
                                     h.tensor_copy(out=o, in_=i),
                                     reads=[self.pbuf(pbank)], writes=[b_pT], tag="ptev")
                            for j in range(npk):
                                k2 = bkk * 4 + j
                                self.mm(self.ps_ap(5, 0, 128),
                                        pT[:, j * 128:(j + 1) * 128],
                                        v2[:, k2 * 256 + hh * 128: k2 * 256 + hh * 128 + 128],
                                        k2 == 0, k2 == nkt - 1,
                                        [b_v2, b_pT], [self.pbuf(5)], "av")
                        self.dve(lambda h, o=colsA[:, 14:15], i=colsA[:, 10:10 + npc]:
                                 h.reduce_sum(o, i, axis=AX.X),
                                 reads=[b_colsA], writes=[b_colsA], tag="lsum")
                        self.dve(lambda h, o=colsA[:, 15:16], i=colsA[:, 14:15]:
                                 h.reciprocal(out=o, in_=i),
                                 reads=[b_colsA], writes=[b_colsA], tag="linv")
                        self.dve(lambda h, o=avsb[:], i=self.ps_ap(5, 0, 128), sc=colsA[:, 15:16]:
                                 h.tensor_scalar(out=o, in0=i, scalar1=sc, scalar2=None, op0=ALU.mult),
                                 reads=[self.pbuf(5), b_colsA], writes=[b_avsb], tag="avsc")
                        self.mk.op("pe", lambda h, o=self.ps_ap(5).bitcast(BF16)[:, 0:128],
                                   i=avsb[:], dd=idb[:]: h.transpose(o, i, dd),
                                   reads=[b_avsb, b_idb], writes=[self.pbuf(5)], tag="avtr")
                        self.dve(lambda h, o=aall[:, slot * TQ + qt * 128: slot * TQ + (qt + 1) * 128],
                                 i=self.ps_ap(5).bitcast(BF16)[:, 0:128]:
                                 h.tensor_copy(out=o, in_=i),
                                 reads=[self.pbuf(5)], writes=[b_aall], tag="avout")
            wslabs = []
            _so = self._stage_on
            self._stage_on = False
            for hc in range(4):
                wslabs.append(self.wload(i_woa[pz * 4 + hc], D, b_woa, tag=f"woa{pz}.{hc}"))
            self._stage_on = _so
            for (t0c, tn) in QB3:
                for o_ in range(KC):
                    bank = o_ % 2
                    for hc in range(4):
                        wt, bw = wslabs[hc]
                        self.mm(self.ps_ap(bank, 0, tn), wt[:, o_ * 128:(o_ + 1) * 128],
                                aall[:, hc * TQ + t0c: hc * TQ + t0c + tn],
                                hc == 0, hc == 3, [bw, b_aall], [self.pbuf(bank)], "wo")
                    self.dve(lambda h, o=S_[:, o_ * TQ + t0c: o_ * TQ + t0c + tn],
                             i=self.ps_ap(bank, 0, tn):
                             h.scalar_tensor_tensor(out=o, in0=i, scalar=1.0, in1=o,
                                                    op0=ALU.mult, op1=ALU.add),
                             reads=[self.pbuf(bank), b_S], writes=[b_S], tag="wo.acc")
        self.stage_pump(len(self.stage_q))
        self.stage_q = self.stage_q2
        self._stage_on = False
        sc3.close()
        sc_q.close()
        sc_kv.close()
        mk.barrier()
        if NPHASE < 4:
            return self.finish(block, qs, ms, S_, b_S, o_y, b_y)

        self.emit_ffn(S_, b_S, TQ, QB3, i_wrsg, b_wrsg, i_wg, b_wg, i_wu, b_wu,
                      i_wd, b_wd, i_sg, b_sg, i_su, b_su, i_sd, b_sd,
                      idf, b_idf, onesf, self.b_onesf, colsA, b_colsA, lrow, b_lrow, "ffn1")
        mk.barrier()
        if NPHASE < 5:
            return self.finish(block, qs, ms, S_, b_S, o_y, b_y)

        self._stage_on = True
        sc5 = contextlib.ExitStack()
        kvl, b_kvl = self.sb(sc5, "kvl", [128, KVLC * TQ], BF16)
        qab, b_qab = self.sb(sc5, "qab", [128, H * TOWN], BF16)
        sc5a = contextlib.ExitStack()
        self._ln_scratch(sc5a, "p4")
        xn3, b_xn3 = self.sb(sc5a, "xn3", [128, KC * TQ], BF16)
        for (t0c, tn) in QB3:
            self.emit_ln_block(
                tn,
                lambda c, t0c=t0c, tn=tn: (S_[:, c * TQ + t0c: c * TQ + t0c + tn], b_S, False),
                lambda c, t0c=t0c, tn=tn: (xn3[:, c * TQ + t0c: c * TQ + t0c + tn], b_xn3),
                f"ln3.{t0c}")
        for oc in range(KVLC):
            wt, bw = self.wload(i_wkvb2[oc], D, b_wkvb2, tag=f"wkv2{oc}")
            for (t0c, tn) in QB3:
                bank = oc % 2
                for kcc in range(KC):
                    self.mm(self.ps_ap(bank, 0, tn), wt[:, kcc * 128:(kcc + 1) * 128],
                            xn3[:, kcc * TQ + t0c: kcc * TQ + t0c + tn],
                            kcc == 0, kcc == KC - 1, [bw, b_xn3], [self.pbuf(bank)], "kvl")
                self.act(lambda h, o=kvl[:, oc * TQ + t0c: oc * TQ + t0c + tn],
                         i=self.ps_ap(bank, 0, tn): h.copy(out=o, in_=i),
                         reads=[self.pbuf(bank)], writes=[b_kvl], tag="kvl.ev")
        for head in range(H):
            wt, bw = self.wload(i_wqbb[head], D, b_wqbb, tag=f"wqb2{head}")
            for (t0c, tn) in QB2:
                bank = 2 + (head % 2)
                for kcc in range(KC):
                    self.mm(self.ps_ap(bank, 0, tn), wt[:, kcc * 128:(kcc + 1) * 128],
                            xn3[:, kcc * TQ + t0c: kcc * TQ + t0c + tn],
                            kcc == 0, kcc == KC - 1, [bw, b_xn3], [self.pbuf(bank)], "qb")
                self.act(lambda h, o=qab[:, head * TOWN + t0c: head * TOWN + t0c + tn],
                         i=self.ps_ap(bank, 0, tn): h.mul(out=o, in_=i, mul=C1),
                         reads=[self.pbuf(bank)], writes=[b_qab], tag="qb.ev")
        sc5a.close()
        mk.barrier()

        sc5b = contextlib.ExitStack()
        bmk, b_bmk = self.sb(sc5b, "bmk", [128, 8 * 384], F32)
        v4, b_v4 = self.sb(sc5b, "v4", [128, 10 * 512], BF16)
        kh, b_kh = self.sb(sc5b, "kh", [128, TQ], BF16)
        aab, b_aab = self.sb(sc5b, "aab", [128, 4 * TOWN], BF16)
        prb, b_prb = self.sb(sc5b, "prb", [128, 384], BF16)
        sq3, b_sq3 = self.sb(sc5b, "sq3", [128, 512], BF16)
        nbrow2, b_nbrow2 = self.sb(sc5b, "nbrow2", [1, TOWN], F32)
        krow2, b_krow2 = self.sb(sc5b, "krow2", [1, 8], F32)
        avsb2, b_avsb2 = self.sb(sc5b, "avsb2", [128, 128], BF16)
        pTb, b_pTb = self.sb(sc5b, "pTb", [128, 384], BF16)
        mk.dma("sp", lambda h: h.dma_start(out=bmk[:], in_=i_bmask),
               reads=[b_bmask], writes=[b_bmk], tag="ld_bmask")

        for pz in range(4):
            g4 = pz
            wt, bw = self.wload3(i_wvb[g4], KVLC, 512, tag=f"wvb{g4}")
            for tc in range(10):
                bank = 5 + tc % 2
                for kcc in range(KVLC):
                    self.mm(self.ps_ap(bank), kvl[:, kcc * TQ + tc * 128: kcc * TQ + (tc + 1) * 128],
                            wt[:, kcc * 512:(kcc + 1) * 512],
                            kcc == 0, kcc == KVLC - 1, [bw, b_kvl], [self.pbuf(bank)], "vb")
                self.dve(lambda h, o=v4[:, tc * 512:(tc + 1) * 512], i=self.ps_ap(bank):
                         h.tensor_copy(out=o, in_=i),
                         reads=[self.pbuf(bank)], writes=[b_v4], tag="vb.ev")
            for hh in range(4):
                head = g4 * 4 + hh
                wt, bw = self.wload(i_wkb[head], KVL, b_wkb, tag=f"wkb{head}")
                for kbi, (t0c, tn) in enumerate(QB3):
                    bank = 5 + kbi % 2
                    for kcc in range(KVLC):
                        self.mm(self.ps_ap(bank, 0, tn), wt[:, kcc * 128:(kcc + 1) * 128],
                                kvl[:, kcc * TQ + t0c: kcc * TQ + t0c + tn],
                                kcc == 0, kcc == KVLC - 1, [bw, b_kvl], [self.pbuf(bank)], "kb")
                    self.act(lambda h, o=kh[:, t0c:t0c + tn], i=self.ps_ap(bank, 0, tn):
                             h.mul(out=o, in_=i, mul=C1),
                             reads=[self.pbuf(bank)], writes=[b_kh], tag="kb.ev")
                for kbi, (t0c, tn) in enumerate(QB3):
                    bank = 5 + kbi % 2
                    self.act(lambda h, o=sq3[:, 0:tn], i=kh[:, t0c:t0c + tn]:
                             h.activation(out=o, in_=i, func=AF.Square),
                             reads=[b_kh], writes=[b_sq3], tag="nb2.ksq")
                    self.mm(self.ps_ap(bank, 0, tn, prows=1), self.onesb[:], sq3[:, 0:tn],
                            True, True, [self.b_onesb, b_sq3], [self.pbuf(bank)], "nb2.kmm")
                    self.dve(lambda h, o=krow2[0:1, kbi:kbi + 1], i=self.ps_ap(bank, 0, tn, prows=1):
                             h.reduce_max(o, i, axis=AX.X),
                             reads=[self.pbuf(bank)], writes=[b_krow2], tag="nb2.kmx")
                self.dve(lambda h, o=krow2[0:1, 4:5], i=krow2[0:1, 0:3]:
                         h.reduce_max(o, i, axis=AX.X),
                         reads=[b_krow2], writes=[b_krow2], tag="nb2.kmx2")
                for qbi, (t0c, tn) in enumerate(QB2):
                    bank = 5 + qbi % 2
                    self.act(lambda h, o=sq3[:, 0:tn], i=qab[:, head * TOWN + t0c: head * TOWN + t0c + tn]:
                             h.activation(out=o, in_=i, func=AF.Square),
                             reads=[b_qab], writes=[b_sq3], tag="nb2.qsq")
                    self.mm(self.ps_ap(bank, 0, tn, prows=1), self.onesb[:], sq3[:, 0:tn],
                            True, True, [self.b_onesb, b_sq3], [self.pbuf(bank)], "nb2.qmm")
                    self.dve(lambda h, o=nbrow2[0:1, t0c:t0c + tn], i=self.ps_ap(bank, 0, tn, prows=1),
                             sc=krow2[0:1, 4:5]:
                             h.tensor_scalar(out=o, in0=i, scalar1=sc, scalar2=0.5,
                                             op0=ALU.add, op1=ALU.mult),
                             reads=[self.pbuf(bank), b_krow2], writes=[b_nbrow2], tag="nb2.q2")
                for qt in range(8):
                    bank = 5 + qt % 2
                    self.mk.op("pe", lambda h, o=self.ps_ap(bank, 0, 1), i=nbrow2[0:1, qt * 128:(qt + 1) * 128],
                               dd=self.id1[:]: h.transpose(o, i, dd),
                               reads=[b_nbrow2, self.b_id1], writes=[self.pbuf(bank)], tag="nb2.tr")
                    self.dve(lambda h, o=colsA[:, qt:qt + 1], i=self.ps_ap(bank, 0, 1):
                             h.tensor_scalar(out=o, in0=i, scalar1=-1.0, scalar2=None, op0=ALU.mult),
                             reads=[self.pbuf(bank)], writes=[b_colsA], tag="nb2.ev")
                for qt in range(8):
                    bank = qt % 2
                    qsl = qab[:, head * TOWN + qt * 128: head * TOWN + (qt + 1) * 128]
                    if qt == 0:
                        self.mm(self.ps_ap(bank, 0, 256), qsl, kh[:, 1024:1280],
                                True, False, [b_qab, b_kh], [self.pbuf(bank)], "sb0")
                        self.mm(self.ps_ap(bank, 256, 384), qsl, kh[:, 0:128],
                                False, True, [b_qab, b_kh], [self.pbuf(bank)], "sb1")
                    else:
                        self.mm(self.ps_ap(bank, 0, 128), qsl, kh[:, 1024:1152],
                                True, False, [b_qab, b_kh], [self.pbuf(bank)], "sb0")
                        self.mm(self.ps_ap(bank, 128, 384), qsl,
                                kh[:, (qt - 1) * 128:(qt + 1) * 128],
                                False, True, [b_qab, b_kh], [self.pbuf(bank)], "sb1")
                    self.dve(lambda h, o=self.ps_ap(bank, 0, 384), m=bmk[:, qt * 384:(qt + 1) * 384]:
                             h.tensor_tensor(out=o, in0=o, in1=m, op=ALU.add),
                             reads=[self.pbuf(bank), b_bmk], writes=[self.pbuf(bank)], tag="bm")
                    self.act(lambda h, o=prb[:], i=self.ps_ap(bank, 0, 384), bb=colsA[:, qt:qt + 1],
                             ac=colsA[:, 13:14]:
                             h.activation(out=o, in_=i, func=AF.Exp, bias=bb, accum_out=ac),
                             reads=[self.pbuf(bank), b_colsA], writes=[b_prb, b_colsA], tag="bexp")
                    self.dve(lambda h, o=colsA[:, 14:15], i=colsA[:, 13:14]:
                             h.reciprocal(out=o, in_=i),
                             reads=[b_colsA], writes=[b_colsA], tag="blinv")
                    for sl in range(3):
                        self.mk.op("pe", lambda h, o=self.ps_ap(4).bitcast(BF16)[:, sl * 128:(sl + 1) * 128],
                                   i=prb[:, sl * 128:(sl + 1) * 128], dd=idb[:]:
                                   h.transpose(o, i, dd),
                                   reads=[b_prb, b_idb], writes=[self.pbuf(4)], tag="bptr")
                    self.dve(lambda h, o=pTb[:], i=self.ps_ap(4).bitcast(BF16)[:, 0:384]:
                             h.tensor_copy(out=o, in_=i),
                             reads=[self.pbuf(4)], writes=[b_pTb], tag="bptev")
                    vtiles = [8, 9 if qt == 0 else qt - 1, qt]
                    for sl in range(3):
                        vt = vtiles[sl]
                        self.mm(self.ps_ap(5, 0, 128),
                                pTb[:, sl * 128:(sl + 1) * 128],
                                v4[:, vt * 512 + hh * 128: vt * 512 + hh * 128 + 128],
                                sl == 0, sl == 2, [b_v4, b_pTb], [self.pbuf(5)], "bav")
                    self.dve(lambda h, o=avsb2[:], i=self.ps_ap(5, 0, 128), sc=colsA[:, 14:15]:
                             h.tensor_scalar(out=o, in0=i, scalar1=sc, scalar2=None, op0=ALU.mult),
                             reads=[self.pbuf(5), b_colsA], writes=[b_avsb2], tag="bavsc")

                    self.mk.op("pe", lambda h, o=self.ps_ap(5).bitcast(BF16)[:, 0:128],
                               i=avsb2[:], dd=idb[:]: h.transpose(o, i, dd),
                               reads=[b_avsb2, b_idb], writes=[self.pbuf(5)], tag="bavtr")
                    self.dve(lambda h, o=aab[:, hh * TOWN + qt * 128: hh * TOWN + (qt + 1) * 128],
                             i=self.ps_ap(5).bitcast(BF16)[:, 0:128]:
                             h.tensor_copy(out=o, in_=i),
                             reads=[self.pbuf(5)], writes=[b_aab], tag="bavout")
            wslabs = []
            for hc in range(4):
                wslabs.append(self.wload(i_wob[pz * 4 + hc], D, b_wob, tag=f"wob{pz}.{hc}"))
            for (t0c, tn) in QB2:
                for o_ in range(KC):
                    bank = 2 + o_ % 2
                    for hc in range(4):
                        wt, bw = wslabs[hc]
                        self.mm(self.ps_ap(bank, 0, tn), wt[:, o_ * 128:(o_ + 1) * 128],
                                aab[:, hc * TOWN + t0c: hc * TOWN + t0c + tn],
                                hc == 0, hc == 3, [bw, b_aab], [self.pbuf(bank)], "wob")
                    self.dve(lambda h, o=S_[:, o_ * TQ + t0c: o_ * TQ + t0c + tn],
                             i=self.ps_ap(bank, 0, tn):
                             h.scalar_tensor_tensor(out=o, in0=i, scalar=1.0, in1=o,
                                                    op0=ALU.mult, op1=ALU.add),
                             reads=[self.pbuf(bank), b_S], writes=[b_S], tag="wob.acc")
        self.stage_pump(len(self.stage_q))
        self._stage_on = False
        sc5b.close()
        sc5.close()
        mk.barrier()
        if NPHASE < 6:
            return self.finish(block, qs, ms, S_, b_S, o_y, b_y)

        self.emit_ffn(S_, b_S, TOWN, QB2, i_wrsg, b_wrsg, i_wg, b_wg, i_wu, b_wu,
                      i_wd, b_wd, i_sg, b_sg, i_su, b_su, i_sd, b_sd,
                      idf, b_idf, onesf, self.b_onesf, colsA, b_colsA, lrow, b_lrow, "ffn2")
        mk.barrier()
        return self.finish(block, qs, ms, S_, b_S, o_y, b_y)

    def emit_ffn(self, S_, b_S, T, TBLKS, i_wrsg, b_wrsg, i_wg, b_wg, i_wu, b_wu,
                 i_wd, b_wd, i_sg, b_sg, i_su, b_su, i_sd, b_sd,
                 idf, b_idf, onesf, b_onesf, colsA, b_colsA, lrow, b_lrow, tag):
        mk = self.mk
        ntile = T // 128
        sr = self.sparse_res
        d_xnd, b_xnd = sr["d_xnd"], sr["b_xnd"]
        d_cdram, b_cdram = sr["d_cdram"], sr["b_cdram"]
        d_idxd, b_idxd = sr["d_idxd"], sr["b_idxd"]
        d_ye, b_ye = sr["d_ye"], sr["b_ye"]
        rb_x = self.rb_tq if T == TQ else self.rb_own

        scf = contextlib.ExitStack()
        r2m, b_r2m = self.sb(scf, tag + "r2m", [1, T], BF16)
        r2r, b_r2r = self.sb(scf, tag + "r2r", [1, T], BF16)
        rowf, b_rowf = self.sb(scf, tag + "rowf", [1, 128], F32)
        wrsg, b_wrsg_s = self.sb(scf, tag + "wrsg", [128, KC * 9], F32)
        onescf, b_onescf = self.sb(scf, tag + "onescf", [128, 1], F32)
        csb, b_csb = self.sb(scf, tag + "csb", [128, 9], F32)
        Cmat, b_Cmat = self.sb(scf, tag + "Cmat", [128, ntile * 8], F32)
        rcol, b_rcol = self.sb(scf, tag + "rcol", [128, 1], F32)
        mcol, b_mcol = self.sb(scf, tag + "mcol", [128, 1], F32)
        t1s, b_t1s = self.sb(scf, tag + "t1s", [128, 9], F32)
        lg, b_lg = self.sb(scf, tag + "lg", [128, 9], F32)
        pr8, b_pr8 = self.sb(scf, tag + "pr8", [128, 8], F32)
        eqm, b_eqm = self.sb(scf, tag + "eqm", [128, 8], F32)
        crw, b_crw = self.sb(scf, tag + "crw", [1, 128], F32)
        sgl, b_sgl = self.sb(scf, tag + "sgl", [128, 512], F32)
        ubf, b_ubf = self.sb(scf, tag + "ubf", [128, 512], F32)
        mskt8, b_mskt8 = self.sb(scf, tag + "mskt8", [128, ntile * 8], BF16)
        sumsrow, b_sumsrow = self.sb(scf, tag + "sumsrow", [1, ntile * 8], F32)
        scast, b_scast = self.sb(scf, tag + "scast", [80, 1], BF16)
        basesc, b_basesc = self.sb(scf, tag + "basesc", [80, 1], F32)
        bcola, b_bcola = self.sb(scf, tag + "bcola", [128, ntile * 8], F32)
        rk8, b_rk8 = self.sb(scf, tag + "rk8", [128, 8], F32)
        slc, b_slc = self.sb(scf, tag + "slc", [128, 8], F32)
        tov, b_tov = self.sb(scf, tag + "tov", [128, 8], F32)
        umb, b_umb = self.sb(scf, tag + "umb", [128, 8], F32)
        ecap, b_ecap = self.sb(scf, tag + "ecap", [128, 8], F32)
        sc01, b_sc01 = self.sb(scf, tag + "sc01", [128, 2], F32)
        sidx, b_sidx = self.sb(scf, tag + "sidx", [128, 2 * ntile], I32)
        posib = [self.sb(scf, tag + f"posib{i}", [128, 8], I32) for i in range(2)]
        xtmbuf, b_xtmbuf = self.sb(scf, tag + "xtmbuf", [128, D], BF16)
        zidx, b_zidx = self.sb(scf, tag + "zidx", [128, CAP // 128], I32)
        for e in range(NE):
            self.dve(lambda h, o=ecap[:, e:e + 1], v=float(e * CAP): h.memset(o, v),
                     [], [b_ecap], tag + ".ec")

        mk.dma("sp", lambda h, o=wrsg: h.dma_start(out=o[:], in_=i_wrsg),
               reads=[b_wrsg], writes=[b_wrsg_s], tag=tag + ".ldwr")
        self.dve(lambda h, o=onescf[:]: h.memset(o, 1.0), [], [b_onescf], tag + ".o1")
        self.dve(lambda h, o=zidx[:]: h.memset(o, 100000), [], [b_zidx], tag + ".zi")

        scx = contextlib.ExitStack()
        xn, b_xn = self.sb(scx, tag + "xn", [128, KC * T], BF16)
        he, b_he = self.sb(scx, tag + "he", [128, FEC * T], BF16)
        sgrep, b_sgrep = self.sb(scx, tag + "sgrep", [128, T], F32)
        scL = contextlib.ExitStack()
        self._ln_scratch(scL, tag)
        for (t0c, tn) in TBLKS:
            self.emit_ln_block(
                tn,
                lambda c, t0c=t0c, tn=tn: (S_[:, c * TQ + t0c: c * TQ + t0c + tn], b_S, False),
                lambda c, t0c=t0c, tn=tn: (xn[:, c * T + t0c: c * T + t0c + tn], b_xn),
                tag + ".ln")

        for kcc in range(KC):
            self.mm(self.ps_ap(6, 0, 9, prows=1), onescf[:], wrsg[:, kcc * 9:(kcc + 1) * 9],
                    kcc == 0, kcc == KC - 1, [b_onescf, b_wrsg_s], [self.pbuf(6)], tag + ".cs")
        self.dve(lambda h, o=lrow[0:1, 0:9], i=self.ps_ap(6, 0, 9, prows=1):
                 h.tensor_copy(out=o, in_=i), reads=[self.pbuf(6)], writes=[b_lrow], tag=tag + ".cs2")
        self.mm(self.ps_ap(6, 16, 25), onesf[:], lrow[0:1, 0:9], True, True,
                [b_onesf, b_lrow], [self.pbuf(6)], tag + ".cs3")
        self.dve(lambda h, o=csb[:], i=self.ps_ap(6, 16, 25):
                 h.tensor_copy(out=o, in_=i), reads=[self.pbuf(6)], writes=[b_csb], tag=tag + ".cs4")

        for bi, (t0c, tn) in enumerate(TBLKS):
            for c in range(KC):
                xb, bxb = self.xcast[c % 2]
                self.dve(lambda h, o=xb[:, 0:tn], i=S_[:, c * TQ + t0c: c * TQ + t0c + tn]:
                         h.tensor_copy(out=o, in_=i),
                         reads=[b_S], writes=[bxb], tag=tag + ".rc")
                sq, bsq = self.sqb[c % 2]
                self.act(lambda h, o=sq[:, 0:tn], i=xb[:, 0:tn]:
                         h.activation(out=o, in_=i, func=AF.Square),
                         reads=[bxb], writes=[bsq], tag=tag + ".rsq")
                self.mm(self.ps_ap(6, 0, tn, prows=1), self.onesb[:], xb[:, 0:tn],
                        c == 0, c == KC - 1, [self.b_onesb, bxb], [self.pbuf(6)], tag + ".rmm")
                self.mm(self.ps_ap(7, 0, tn, prows=1), self.onesb[:], sq[:, 0:tn],
                        c == 0, c == KC - 1, [self.b_onesb, bsq], [self.pbuf(7)], tag + ".rms")
            rm, b_rm = self.rows_m
            rs, b_rs = self.rows_ms
            rr, b_rr = self.rows_r
            self.act(lambda h, o=rm[0:1, 0:tn], i=self.ps_ap(6, 0, tn, prows=1):
                     h.mul(out=o, in_=i, mul=1.0 / D),
                     reads=[self.pbuf(6)], writes=[b_rm], tag=tag + ".r1")
            self.dve(lambda h, o=r2m[0:1, t0c:t0c + tn], i=rm[0:1, 0:tn]:
                     h.tensor_copy(out=o, in_=i),
                     reads=[b_rm], writes=[b_r2m], tag=tag + ".r1b")
            self.act(lambda h, o=rs[0:1, 0:tn], i=self.ps_ap(7, 0, tn, prows=1):
                     h.mul(out=o, in_=i, mul=1.0 / D),
                     reads=[self.pbuf(7)], writes=[b_rs], tag=tag + ".r2")
            self.dve(lambda h, o=rr[0:1, 0:tn], a=rm[0:1, 0:tn]:
                     h.tensor_mul(out=o, in0=a, in1=a),
                     reads=[b_rm], writes=[b_rr], tag=tag + ".r3")
            self.dve(lambda h, o=rs[0:1, 0:tn], a=rs[0:1, 0:tn], b=rr[0:1, 0:tn]:
                     h.tensor_sub(out=o, in0=a, in1=b),
                     reads=[b_rs, b_rr], writes=[b_rs], tag=tag + ".r4")
            self.act(lambda h, o=rs[0:1, 0:tn], i=rs[0:1, 0:tn], bb=self.epsb[0:1, 0:1]:
                     h.activation(out=o, in_=i, func=AF.Sqrt, bias=bb),
                     reads=[b_rs, self.b_epsb], writes=[b_rs], tag=tag + ".r5")
            self.dve(lambda h, o=rs[0:1, 0:tn], i=rs[0:1, 0:tn]:
                     h.reciprocal(out=o, in_=i),
                     reads=[b_rs], writes=[b_rs], tag=tag + ".r5b")
            self.dve(lambda h, o=r2r[0:1, t0c:t0c + tn], i=rs[0:1, 0:tn]:
                     h.tensor_copy(out=o, in_=i),
                     reads=[b_rs], writes=[b_r2r], tag=tag + ".r6")

        for tt in range(ntile):
            for kcc in range(KC):
                self.mm(self.ps_ap(7, 0, 9), S_[:, kcc * TQ + tt * 128: kcc * TQ + (tt + 1) * 128],
                        wrsg[:, kcc * 9:(kcc + 1) * 9],
                        kcc == 0, kcc == KC - 1, [b_S, b_wrsg_s], [self.pbuf(7)], tag + ".lg")
            self.dve(lambda h, o=rowf[0:1, :], i=r2r[0:1, tt * 128:(tt + 1) * 128]:
                     h.tensor_copy(out=o, in_=i),
                     reads=[b_r2r], writes=[b_rowf], tag=tag + ".trf")
            self.mk.op("pe", lambda h, o=self.ps_ap(6, 0, 1), i=rowf[0:1, :],
                       dd=self.id1[:]: h.transpose(o, i, dd),
                       reads=[b_rowf, self.b_id1], writes=[self.pbuf(6)], tag=tag + ".tr")
            self.dve(lambda h, o=rcol[:], i=self.ps_ap(6, 0, 1): h.tensor_copy(out=o, in_=i),
                     reads=[self.pbuf(6)], writes=[b_rcol], tag=tag + ".trc")
            self.dve(lambda h, o=rowf[0:1, :], i=r2m[0:1, tt * 128:(tt + 1) * 128]:
                     h.tensor_copy(out=o, in_=i),
                     reads=[b_r2m], writes=[b_rowf], tag=tag + ".tmf")
            self.mk.op("pe", lambda h, o=self.ps_ap(6, 4, 5), i=rowf[0:1, :],
                       dd=self.id1[:]: h.transpose(o, i, dd),
                       reads=[b_rowf, self.b_id1], writes=[self.pbuf(6)], tag=tag + ".tm")
            self.dve(lambda h, o=mcol[:], i=self.ps_ap(6, 4, 5): h.tensor_copy(out=o, in_=i),
                     reads=[self.pbuf(6)], writes=[b_mcol], tag=tag + ".tmc")
            self.dve(lambda h, o=t1s[:], i=csb[:], s=mcol[:, 0:1]:
                     h.tensor_scalar(out=o, in0=i, scalar1=s, scalar2=None, op0=ALU.mult),
                     reads=[b_csb, b_mcol], writes=[b_t1s], tag=tag + ".t1")
            self.dve(lambda h, o=lg[:], a=self.ps_ap(7, 0, 9), b=t1s[:]:
                     h.tensor_sub(out=o, in0=a, in1=b),
                     reads=[self.pbuf(7), b_t1s], writes=[b_lg], tag=tag + ".t2")
            self.dve(lambda h, o=lg[:], i=lg[:], s=rcol[:, 0:1]:
                     h.tensor_scalar(out=o, in0=i, scalar1=s, scalar2=None, op0=ALU.mult),
                     reads=[b_lg, b_rcol], writes=[b_lg], tag=tag + ".t3")
            self.dve(lambda h, o=colsA[:, 0:1], i=lg[:, 0:8]:
                     h.reduce_max(o, i, axis=AX.X, negate=True),
                     reads=[b_lg], writes=[b_colsA], tag=tag + ".sm1")
            self.act(lambda h, o=pr8[:], i=lg[:, 0:8], bb=colsA[:, 0:1], ac=colsA[:, 1:2]:
                     h.activation(out=o, in_=i, func=AF.Exp, bias=bb, accum_out=ac),
                     reads=[b_lg, b_colsA], writes=[b_pr8, b_colsA], tag=tag + ".sm2")
            self.dve(lambda h, o=colsA[:, 2:3], i=colsA[:, 1:2]: h.reciprocal(out=o, in_=i),
                     reads=[b_colsA], writes=[b_colsA], tag=tag + ".sm3")
            self.dve(lambda h, o=pr8[:], i=pr8[:], s=colsA[:, 2:3]:
                     h.tensor_scalar(out=o, in0=i, scalar1=s, scalar2=None, op0=ALU.mult),
                     reads=[b_pr8, b_colsA], writes=[b_pr8], tag=tag + ".sm4")
            self.dve(lambda h, o=colsA[:, 3:4], i=pr8[:]:
                     h.reduce_max(o, i, axis=AX.X),
                     reads=[b_pr8], writes=[b_colsA], tag=tag + ".tk1")
            self.dve(lambda h, o=eqm[:], i=pr8[:], s=colsA[:, 3:4]:
                     h.tensor_scalar(out=o, in0=i, scalar1=s, scalar2=None, op0=ALU.is_ge),
                     reads=[b_pr8, b_colsA], writes=[b_eqm], tag=tag + ".tk2")
            self.dve(lambda h, o=eqm[:], a=eqm[:], b=pr8[:]:
                     h.scalar_tensor_tensor(out=o, in0=a, scalar=1000.0, in1=b,
                                            op0=ALU.mult, op1=ALU.subtract),
                     reads=[b_eqm, b_pr8], writes=[b_eqm], tag=tag + ".tk3")
            self.dve(lambda h, o=colsA[:, 4:5], i=eqm[:]:
                     h.tensor_reduce(o, i, axis=AX.X, op=ALU.min, negate=True),
                     reads=[b_eqm], writes=[b_colsA], tag=tag + ".tk4")
            self.dve(lambda h, o=eqm[:], i=pr8[:], s=colsA[:, 4:5]:
                     h.tensor_scalar(out=o, in0=i, scalar1=s, scalar2=None, op0=ALU.is_ge),
                     reads=[b_pr8, b_colsA], writes=[b_eqm], tag=tag + ".tk5")
            self.dve(lambda h, o=Cmat[:, tt * 8:(tt + 1) * 8], a=pr8[:], b=eqm[:]:
                     h.tensor_mul(out=o, in0=a, in1=b),
                     reads=[b_pr8, b_eqm], writes=[b_Cmat], tag=tag + ".tk6")
            self.act(lambda h, o=lg[:, 8:9], i=lg[:, 8:9]:
                     h.activation(out=o, in_=i, func=AF.Sigmoid),
                     reads=[b_lg], writes=[b_lg], tag=tag + ".sg1")
            self.mk.op("pe", lambda h, o=self.ps_ap(6, 8, 136, prows=1),
                       i=lg[:, 8:9], dd=idf[:]: h.transpose(o, i, dd),
                       reads=[b_lg, b_idf], writes=[self.pbuf(6)], tag=tag + ".sg2")
            self.dve(lambda h, o=crw[0:1, :], i=self.ps_ap(6, 8, 136, prows=1):
                     h.tensor_copy(out=o, in_=i),
                     reads=[self.pbuf(6)], writes=[b_crw], tag=tag + ".sg3")
            self.mm(self.ps_ap(6, 256, 384), onesf[:], crw[0:1, :], True, True,
                    [b_onesf, b_crw], [self.pbuf(6)], tag + ".sg4")
            self.dve(lambda h, o=sgrep[:, tt * 128:(tt + 1) * 128], i=self.ps_ap(6, 256, 384):
                     h.tensor_copy(out=o, in_=i),
                     reads=[self.pbuf(6)], writes=[b_sgrep], tag=tag + ".sg5")
        scL.close()

        for tt in range(ntile):
            for half in range(2):
                for c8 in range(8):
                    c = half * 8 + c8
                    self.mk.op("pe", lambda h, o=self.ps_ap(4 + half).bitcast(BF16)[:, c8 * 128:(c8 + 1) * 128],
                               i=xn[:, c * T + tt * 128: c * T + (tt + 1) * 128], dd=self.idb_t[:]:
                               h.transpose(o, i, dd),
                               reads=[b_xn, self.b_idb], writes=[self.pbuf(4 + half)], tag=tag + ".xtp")
                self.dve(lambda h, o=xtmbuf[:, half * 1024:(half + 1) * 1024],
                         i=self.ps_ap(4 + half).bitcast(BF16)[:, 0:1024]:
                         h.tensor_copy(out=o, in_=i),
                         reads=[self.pbuf(4 + half)], writes=[b_xtmbuf], tag=tag + ".xte")
            mk.dma("sp", lambda h, tt=tt, xb=xtmbuf: h.dma_start(
                out=d_xnd[tt * 128:(tt + 1) * 128, :], in_=xb[:]),
                reads=[b_xtmbuf], writes=[b_xnd], tag=tag + ".xts")
        mk.dma("sp", lambda h: h.dma_start(
            out=d_cdram[0:T, :].rearrange("(tt p) e -> p tt e", p=128),
            in_=Cmat[:].rearrange("p (tt e) -> p tt e", e=8)),
            reads=[b_Cmat], writes=[b_cdram], tag=tag + ".cds")

        for tt in range(ntile):
            self.dve(lambda h, o=mskt8[:, tt * 8:(tt + 1) * 8], i=Cmat[:, tt * 8:(tt + 1) * 8]:
                     h.tensor_scalar(out=o, in0=i, scalar1=0.0, scalar2=None, op0=ALU.is_gt),
                     reads=[b_Cmat], writes=[b_mskt8], tag=tag + ".msk")
            self.mm(self.ps_ap(2, tt * 8, (tt + 1) * 8, prows=1), self.onesb[:],
                    mskt8[:, tt * 8:(tt + 1) * 8],
                    True, True, [self.b_onesb, b_mskt8], [self.pbuf(2)], tag + ".sum")
        self.dve(lambda h, o=sumsrow[0:1, :], i=self.ps_ap(2, 0, ntile * 8, prows=1):
                 h.tensor_copy(out=o, in_=i),
                 reads=[self.pbuf(2)], writes=[b_sumsrow], tag=tag + ".sume")
        self.mk.op("pe", lambda h, o=self.ps_ap(3, 0, 1, prows=ntile * 8), i=sumsrow[0:1, :],
                   dd=self.id1[:]: h.transpose(o, i, dd),
                   reads=[b_sumsrow, self.b_id1], writes=[self.pbuf(3)], tag=tag + ".str")
        self.dve(lambda h, o=scast[0:ntile * 8, :], i=self.ps_ap(3, 0, 1, prows=ntile * 8):
                 h.tensor_copy(out=o, in_=i),
                 reads=[self.pbuf(3)], writes=[b_scast], tag=tag + ".stc")
        self.mm(self.ps_ap(3, 4, 5, prows=ntile * 8), self.tri8b[0:ntile * 8, 0:ntile * 8],
                scast[0:ntile * 8, :], True, True,
                [self.b_tri8b, b_scast], [self.pbuf(3)], tag + ".bmm")
        self.dve(lambda h, o=basesc[0:ntile * 8, :], i=self.ps_ap(3, 4, 5, prows=ntile * 8):
                 h.tensor_copy(out=o, in_=i),
                 reads=[self.pbuf(3)], writes=[b_basesc], tag=tag + ".bev")
        self.mk.op("pe", lambda h, o=self.ps_ap(3, 8, 8 + ntile * 8, prows=1),
                   i=basesc[0:ntile * 8, :], dd=idf[0:ntile * 8, 0:ntile * 8]:
                   h.transpose(o, i, dd),
                   reads=[b_basesc, b_idf], writes=[self.pbuf(3)], tag=tag + ".btr")
        self.dve(lambda h, o=lrow[0:1, 0:ntile * 8], i=self.ps_ap(3, 8, 8 + ntile * 8, prows=1):
                 h.tensor_copy(out=o, in_=i),
                 reads=[self.pbuf(3)], writes=[b_lrow], tag=tag + ".bre")
        self.mm(self.ps_ap(3, 128, 128 + ntile * 8), onesf[:], lrow[0:1, 0:ntile * 8],
                True, True, [b_onesf, b_lrow], [self.pbuf(3)], tag + ".bbc")
        self.dve(lambda h, o=bcola[:], i=self.ps_ap(3, 128, 128 + ntile * 8):
                 h.tensor_copy(out=o, in_=i),
                 reads=[self.pbuf(3)], writes=[b_bcola], tag=tag + ".bce")

        for e in range(NE):
            mk.dma("pool", lambda h, e=e: h.dma_start(
                out=d_idxd[e * CAP:(e + 1) * CAP].rearrange("(p j) o -> p (j o)", p=128),
                in_=zidx[:]),
                reads=[b_zidx], writes=[b_idxd[e]], tag=tag + f".zix{e}")
        for tt in range(ntile):
            self.mm(self.ps_ap(2, 0, 8), self.trib[:], mskt8[:, tt * 8:(tt + 1) * 8],
                    True, True, [self.b_trib, b_mskt8], [self.pbuf(2)], tag + ".rkm")
            self.dve(lambda h, o=rk8[:], i=self.ps_ap(2, 0, 8): h.tensor_copy(out=o, in_=i),
                     reads=[self.pbuf(2)], writes=[b_rk8], tag=tag + ".rke")
            pb, b_pb = posib[tt % 2]
            self.dve(lambda h, o=umb[:], i=mskt8[:, tt * 8:(tt + 1) * 8]:
                     h.tensor_scalar(out=o, in0=i, scalar1=-1000000.0, scalar2=1000000.0,
                                     op0=ALU.mult, op1=ALU.add),
                     reads=[b_mskt8], writes=[b_umb], tag=tag + ".im")
            self.dve(lambda h, o=slc[:], a=rk8[:], b=umb[:]: h.tensor_add(out=o, in0=a, in1=b),
                     reads=[b_rk8, b_umb], writes=[b_slc], tag=tag + ".ip")
            self.dve(lambda h, o=slc[:], a=slc[:], b=bcola[:, tt * 8:(tt + 1) * 8]:
                     h.tensor_add(out=o, in0=a, in1=b),
                     reads=[b_slc, b_bcola], writes=[b_slc], tag=tag + ".ip2")
            self.dve(lambda h, o=pb[:], i=slc[:]: h.tensor_copy(out=o, in_=i),
                     reads=[b_slc], writes=[b_pb], tag=tag + ".ipi")
            for e in range(NE):
                mk.dma("pool", lambda h, e=e, tt=tt, pb=pb: h.indirect_dma_start(
                    out=d_idxd, out_offset=bass.IndirectOffsetOnAxis(ap=pb[:, e:e + 1], axis=0),
                    in_=self.tokids[:, tt:tt + 1], in_offset=None,
                    element_offset=e * CAP,
                    bounds_check=self.rb_cap, oob_is_err=False),
                    reads=[b_pb, self.b_tokids], writes=[b_idxd[e]], tag=tag + f".scx{e}")
            self.dve(lambda h, o=tov[:], i=rk8[:]:
                     h.tensor_scalar(out=o, in0=i, scalar1=float(CAP), scalar2=None,
                                     op0=ALU.is_ge),
                     reads=[b_rk8], writes=[b_tov], tag=tag + ".sov")
            self.dve(lambda h, o=umb[:], i=mskt8[:, tt * 8:(tt + 1) * 8]:
                     h.tensor_scalar(out=o, in0=i, scalar1=-100000.0, scalar2=100000.0,
                                     op0=ALU.mult, op1=ALU.add),
                     reads=[b_mskt8], writes=[b_umb], tag=tag + ".sm0")
            self.dve(lambda h, o=slc[:], a=rk8[:], b=bcola[:, tt * 8:(tt + 1) * 8]:
                     h.tensor_add(out=o, in0=a, in1=b),
                     reads=[b_rk8, b_bcola], writes=[b_slc], tag=tag + ".sp1")
            self.dve(lambda h, o=slc[:], a=slc[:], b=ecap[:]: h.tensor_add(out=o, in0=a, in1=b),
                     reads=[b_slc, b_ecap], writes=[b_slc], tag=tag + ".sp2")
            self.dve(lambda h, o=slc[:], a=tov[:], b=slc[:]:
                     h.scalar_tensor_tensor(out=o, in0=a, scalar=100000.0, in1=b,
                                            op0=ALU.mult, op1=ALU.add),
                     reads=[b_tov, b_slc], writes=[b_slc], tag=tag + ".sp3")
            self.dve(lambda h, o=slc[:], a=slc[:], b=umb[:]: h.tensor_add(out=o, in0=a, in1=b),
                     reads=[b_slc, b_umb], writes=[b_slc], tag=tag + ".sp4")
            self.dve(lambda h, o=sc01[:, 0:1], i=slc[:]:
                     h.tensor_reduce(o, i, axis=AX.X, op=ALU.min),
                     reads=[b_slc], writes=[b_sc01], tag=tag + ".sr0")
            self.dve(lambda h, o=tov[:], i=slc[:], s=sc01[:, 0:1]:
                     h.tensor_scalar(out=o, in0=i, scalar1=s, scalar2=None, op0=ALU.is_le),
                     reads=[b_slc, b_sc01], writes=[b_tov], tag=tag + ".se0")
            self.dve(lambda h, o=slc[:], a=tov[:], b=slc[:]:
                     h.scalar_tensor_tensor(out=o, in0=a, scalar=100000.0, in1=b,
                                            op0=ALU.mult, op1=ALU.add),
                     reads=[b_tov, b_slc], writes=[b_slc], tag=tag + ".se1")
            self.dve(lambda h, o=sc01[:, 1:2], i=slc[:]:
                     h.tensor_reduce(o, i, axis=AX.X, op=ALU.min),
                     reads=[b_slc], writes=[b_sc01], tag=tag + ".sr1")
            self.dve(lambda h, o=sidx[:, 2 * tt:2 * tt + 2], i=sc01[:]:
                     h.tensor_copy(out=o, in_=i),
                     reads=[b_sc01], writes=[b_sidx], tag=tag + ".sic")

        nb = len(TBLKS)
        gbanks = [0, 1, 2][:nb]
        ubanks = [3, 4, 5][:nb]
        wc = self.wcache
        for unit in range(NE, NUNIT):
            for fc in range(FEC):
                k = (unit - NE) * FEC + fc
                gs, bgs = self.wload_c(wc["d_csg"][k], D, wc["b_csg"][k], tag="sgw")
                us, bus = self.wload_c(wc["d_csu"][k], D, wc["b_csu"][k], tag="suw")
                for bi, (t0c, tn) in enumerate(TBLKS):
                    for kcc in range(KC):
                        self.mm(self.ps_ap(gbanks[bi], 0, tn), gs[:, kcc * 128:(kcc + 1) * 128],
                                xn[:, kcc * T + t0c: kcc * T + t0c + tn],
                                kcc == 0, kcc == KC - 1, [bgs, b_xn], [self.pbuf(gbanks[bi])], "g")
                for bi, (t0c, tn) in enumerate(TBLKS):
                    for kcc in range(KC):
                        self.mm(self.ps_ap(ubanks[bi], 0, tn), us[:, kcc * 128:(kcc + 1) * 128],
                                xn[:, kcc * T + t0c: kcc * T + t0c + tn],
                                kcc == 0, kcc == KC - 1, [bus, b_xn], [self.pbuf(ubanks[bi])], "u")
                for bi, (t0c, tn) in enumerate(TBLKS):
                    self.act(lambda h, o=sgl[:, 0:tn], i=self.ps_ap(gbanks[bi], 0, tn):
                             h.activation(out=o, in_=i, func=AF.Silu),
                             reads=[self.pbuf(gbanks[bi])], writes=[b_sgl], tag="silu")
                    self.dve(lambda h, o=ubf[:, 0:tn], a=self.ps_ap(ubanks[bi], 0, tn),
                             b=sgrep[:, t0c:t0c + tn]:
                             h.scalar_tensor_tensor(out=o, in0=a, scalar=1.0, in1=b,
                                                    op0=ALU.mult, op1=ALU.mult),
                             reads=[self.pbuf(ubanks[bi]), b_sgrep], writes=[b_ubf], tag="uev")
                    self.dve(lambda h, o=he[:, fc * T + t0c: fc * T + t0c + tn],
                             a=sgl[:, 0:tn], b=ubf[:, 0:tn]:
                             h.tensor_mul(out=o, in0=a, in1=b),
                             reads=[b_sgl, b_ubf], writes=[b_he], tag="hmul")
            for o_ in range(KC):
                ds, bds = self.wload_c(wc["d_csd"][o_, unit - NE], FE,
                                       wc["b_csd"][o_ * 4 + (unit - NE)], tag="sdw")
                for bi, (t0c, tn) in enumerate(TBLKS):
                    bank = [6, 7, 0, 1][(o_ * nb + bi) % 4]
                    for fc in range(FEC):
                        self.mm(self.ps_ap(bank, 0, tn), ds[:, fc * 128:(fc + 1) * 128],
                                he[:, fc * T + t0c: fc * T + t0c + tn],
                                fc == 0, fc == FEC - 1, [bds, b_he], [self.pbuf(bank)], "d")
                    self.dve(lambda h, o=S_[:, o_ * TQ + t0c: o_ * TQ + t0c + tn],
                             i=self.ps_ap(bank, 0, tn):
                             h.scalar_tensor_tensor(out=o, in0=i, scalar=1.0, in1=o,
                                                    op0=ALU.mult, op1=ALU.add),
                             reads=[self.pbuf(bank), b_S], writes=[b_S], tag="d.acc")
        scx.close()
        mk.barrier()

        sce = contextlib.ExitStack()
        xg2 = [self.sb(sce, tag + f"xg{i}", [128, KC * 512], BF16) for i in range(2)]
        xtile = [self.sb(sce, tag + f"xtile{i}", [128, D], BF16) for i in range(4)]
        ctile = [self.sb(sce, tag + f"ctile{i}", [128, 8], F32) for i in range(4)]
        ytile = [self.sb(sce, tag + f"ytile{i}", [128, D], BF16) for i in range(4)]
        yfm, b_yfm = self.sb(sce, tag + "yfm", [128, 256], BF16)
        crsp2 = [self.sb(sce, tag + f"crsp{i}", [128, 512], F32) for i in range(2)]
        idxc2 = [self.sb(sce, tag + f"idxc{i}", [128, CAP // 128], I32) for i in range(2)]
        cecol, b_cecol = self.sb(sce, tag + "cecol", [128, CAP // 128], F32)
        hes, b_hes = self.sb(sce, tag + "hes", [128, FEC * 512], BF16)

        for i in range(2):
            self.dve(lambda h, o=xg2[i][0][:]: h.memset(o, 0.0), [], [xg2[i][1]], tag + ".xg0")
        for i in range(4):
            self.dve(lambda h, o=xtile[i][0][:]: h.memset(o, 0.0), [], [xtile[i][1]], tag + ".xt0")

        def egd(e, js):
            idxc, b_idxc = idxc2[e % 2]
            if js[0] == 0:
                mk.dma("pool", lambda h, e=e, ix=idxc: h.dma_start(
                    out=ix[:],
                    in_=d_idxd[e * CAP:(e + 1) * CAP].rearrange("(p j) o -> p (j o)", p=128)),
                    reads=[b_idxd[e]], writes=[b_idxc], tag=tag + f".lix{e}")
            for j in js:
                xt, b_xt = xtile[(e % 2) * 2 + j % 2]
                ct, b_ct = ctile[(e % 2) * 2 + j % 2]
                mk.dma("pool", lambda h, xt=xt, ix=idxc, j=j: h.indirect_dma_start(
                    out=xt[:], out_offset=None,
                    in_=d_xnd, in_offset=bass.IndirectOffsetOnAxis(ap=ix[:, j:j + 1], axis=0),
                    bounds_check=rb_x, oob_is_err=False),
                    reads=[b_idxc, b_xnd], writes=[b_xt], tag=tag + ".gx")
                mk.dma("pool", lambda h, ct=ct, ix=idxc, j=j: h.indirect_dma_start(
                    out=ct[:], out_offset=None,
                    in_=d_cdram, in_offset=bass.IndirectOffsetOnAxis(ap=ix[:, j:j + 1], axis=0),
                    bounds_check=rb_x, oob_is_err=False),
                    reads=[b_idxc, b_cdram], writes=[b_ct], tag=tag + ".gc")

        def etp(e, j):
            xt, b_xt = xtile[(e % 2) * 2 + j % 2]
            ct, b_ct = ctile[(e % 2) * 2 + j % 2]
            xg, b_xg = xg2[e % 2]
            crsp, b_crsp = crsp2[e % 2]
            for half in range(2):
                for c8 in range(8):
                    c = half * 8 + c8
                    self.mk.op("pe", lambda h, o=self.ps_ap(4).bitcast(BF16)[:, c8 * 128:(c8 + 1) * 128],
                               i=xt[:, c * 128:(c + 1) * 128], dd=self.idb_t[:]:
                               h.transpose(o, i, dd),
                               reads=[b_xt, self.b_idb], writes=[self.pbuf(4)], tag=tag + ".gtp")
                self.dve(lambda h, o=xg[:].rearrange("p (c t) -> p c t", c=KC)[:, half * 8:(half + 1) * 8, j * 128:(j + 1) * 128],
                         i=self.ps_ap(4).bitcast(BF16)[:, 0:1024].rearrange("p (c t) -> p c t", c=8):
                         h.tensor_copy(out=o, in_=i),
                         reads=[self.pbuf(4)], writes=[b_xg], tag=tag + ".gte")
            self.dve(lambda h, o=cecol[:, j:j + 1], i=ct[:, e:e + 1]:
                     h.tensor_copy(out=o, in_=i),
                     reads=[b_ct], writes=[b_cecol], tag=tag + ".cec")
            self.mk.op("pe", lambda h, o=self.ps_ap(5, 0, 128, prows=1), i=cecol[:, j:j + 1],
                       dd=idf[:]: h.transpose(o, i, dd),
                       reads=[b_cecol, b_idf], writes=[self.pbuf(5)], tag=tag + ".ctr")
            self.dve(lambda h, o=crw[0:1, :], i=self.ps_ap(5, 0, 128, prows=1):
                     h.tensor_copy(out=o, in_=i),
                     reads=[self.pbuf(5)], writes=[b_crw], tag=tag + ".cre")
            self.mm(self.ps_ap(5, 128, 256), onesf[:], crw[0:1, :], True, True,
                    [b_onesf, b_crw], [self.pbuf(5)], tag + ".cbc")
            self.dve(lambda h, o=crsp[:, j * 128:(j + 1) * 128], i=self.ps_ap(5, 128, 256):
                     h.tensor_copy(out=o, in_=i),
                     reads=[self.pbuf(5)], writes=[b_crsp], tag=tag + ".crv")

        egd(0, (0, 1))
        etp(0, 0)
        etp(0, 1)
        egd(0, (2, 3))
        etp(0, 2)
        etp(0, 3)
        for e in range(NE):
            xg, b_xg = xg2[e % 2]
            crsp, b_crsp = crsp2[e % 2]
            if e + 1 < NE:
                egd(e + 1, (0, 1))
            for fc in range(FEC):
                gs, bgs = self.wload(i_wg[e, fc], D, b_wg, tag="gw")
                us, bus = self.wload(i_wu[e, fc], D, b_wu, tag="uw")
                gbank = [0, 2, 6][fc % 3]
                ubank = [1, 3, 7][fc % 3]
                for kcc in range(KC):
                    self.mm(self.ps_ap(gbank), gs[:, kcc * 128:(kcc + 1) * 128],
                            xg[:, kcc * 512:(kcc + 1) * 512],
                            kcc == 0, kcc == KC - 1, [bgs, b_xg], [self.pbuf(gbank)], "g")
                for kcc in range(KC):
                    self.mm(self.ps_ap(ubank), us[:, kcc * 128:(kcc + 1) * 128],
                            xg[:, kcc * 512:(kcc + 1) * 512],
                            kcc == 0, kcc == KC - 1, [bus, b_xg], [self.pbuf(ubank)], "u")
                self.act(lambda h, o=sgl[:, 0:512], i=self.ps_ap(gbank):
                         h.activation(out=o, in_=i, func=AF.Silu),
                         reads=[self.pbuf(gbank)], writes=[b_sgl], tag="silu")
                self.dve(lambda h, o=ubf[:, 0:512], a=self.ps_ap(ubank), b=crsp[:]:
                         h.scalar_tensor_tensor(out=o, in0=a, scalar=1.0, in1=b,
                                                op0=ALU.mult, op1=ALU.mult),
                         reads=[self.pbuf(ubank), b_crsp], writes=[b_ubf], tag="uev")
                self.dve(lambda h, o=hes[:, fc * 512:(fc + 1) * 512], a=sgl[:, 0:512], b=ubf[:, 0:512]:
                         h.tensor_mul(out=o, in0=a, in1=b),
                         reads=[b_sgl, b_ubf], writes=[b_hes], tag="hmul")
                if e + 1 < NE:
                    if fc == 1:
                        etp(e + 1, 0)
                    elif fc == 3:
                        etp(e + 1, 1)
                        egd(e + 1, (2, 3))
                    elif fc == 6:
                        etp(e + 1, 2)
                    elif fc == 8:
                        etp(e + 1, 3)
            for o_ in range(KC):
                if tag == "ffn2":
                    ds, bds = self.wload_c(wc["d_cwd"][e, o_], FE,
                                           wc["b_cwd"][e * KC + o_], tag="dw")
                else:
                    ds, bds = self.wload(i_wd[e, o_], FE, b_wd, tag="dw")
                for h2 in range(2):
                    bank = [6, 7, 0, 1][(o_ * 2 + h2) % 4]
                    for fc in range(FEC):
                        self.mm(self.ps_ap(bank, 0, 256), ds[:, fc * 128:(fc + 1) * 128],
                                hes[:, fc * 512 + h2 * 256: fc * 512 + h2 * 256 + 256],
                                fc == 0, fc == FEC - 1, [bds, b_hes], [self.pbuf(bank)], "d")
                    self.dve(lambda h, o=yfm[:], i=self.ps_ap(bank, 0, 256):
                             h.tensor_copy(out=o, in_=i),
                             reads=[self.pbuf(bank)], writes=[b_yfm], tag=tag + ".yev")
                    slot = ((o_ * 2 + h2) % 4) * 2
                    for jj in range(2):
                        j4 = h2 * 2 + jj
                        self.mk.op("pe", lambda h, o=self.ps_ap(5).bitcast(BF16)[:, (slot + jj) * 128:(slot + jj + 1) * 128],
                                   i=yfm[:, jj * 128:(jj + 1) * 128], dd=self.idb_t[:]:
                                   h.transpose(o, i, dd),
                                   reads=[b_yfm, self.b_idb], writes=[self.pbuf(5)], tag=tag + ".ytp")
                        self.dve(lambda h, o=ytile[j4][0][:, o_ * 128:(o_ + 1) * 128],
                                 i=self.ps_ap(5).bitcast(BF16)[:, (slot + jj) * 128:(slot + jj + 1) * 128]:
                                 h.tensor_copy(out=o, in_=i),
                                 reads=[self.pbuf(5)], writes=[ytile[j4][1]], tag=tag + ".yte")
            for j4 in range(4):
                mk.dma("sp", lambda h, e=e, j4=j4: h.dma_start(
                    out=d_ye[e * CAP:(e + 1) * CAP, :].rearrange(
                        "(p j) f -> p j f", j=4)[:, j4:j4 + 1, :],
                    in_=ytile[j4][0][:].rearrange("p (j f) -> p j f", j=1)),
                    reads=[ytile[j4][1]], writes=[b_ye], tag=tag + ".yst")

        ysum, b_ysum = self.sb(sce, tag + "ysum", [128, D], F32)
        for tt in range(ntile):
            ga, b_ga = xtile[(tt % 2) * 2]
            gb, b_gb = xtile[(tt % 2) * 2 + 1]
            self.dve(lambda h, o=ga[:]: h.memset(o, 0.0), [], [b_ga], tag + ".cz0")
            self.dve(lambda h, o=gb[:]: h.memset(o, 0.0), [], [b_gb], tag + ".cz1")
            mk.dma("pool", lambda h, tt=tt, ga=ga: h.indirect_dma_start(
                out=ga[:], out_offset=None,
                in_=d_ye, in_offset=bass.IndirectOffsetOnAxis(ap=sidx[:, 2 * tt:2 * tt + 1], axis=0),
                bounds_check=self.rb_ye, oob_is_err=False),
                reads=[b_sidx, b_ye], writes=[b_ga], tag=tag + ".cg0")
            mk.dma("pool", lambda h, tt=tt, gb=gb: h.indirect_dma_start(
                out=gb[:], out_offset=None,
                in_=d_ye, in_offset=bass.IndirectOffsetOnAxis(ap=sidx[:, 2 * tt + 1:2 * tt + 2], axis=0),
                bounds_check=self.rb_ye, oob_is_err=False),
                reads=[b_sidx, b_ye], writes=[b_gb], tag=tag + ".cg1")
            self.dve(lambda h, o=ysum[:], a=ga[:], b=gb[:]: h.tensor_add(out=o, in0=a, in1=b),
                     reads=[b_ga, b_gb], writes=[b_ysum], tag=tag + ".cad")
            for q4 in range(4):
                for c4 in range(4):
                    c = q4 * 4 + c4
                    self.mk.op("pe", lambda h, o=self.ps_ap(4, c4 * 128, (c4 + 1) * 128),
                               i=ysum[:, c * 128:(c + 1) * 128], dd=idf[:]:
                               h.transpose(o, i, dd),
                               reads=[b_ysum, b_idf], writes=[self.pbuf(4)], tag=tag + ".ftp")
                self.dve(lambda h, o=S_[:].rearrange("p (c t) -> p c t", t=TQ)[:, q4 * 4:(q4 + 1) * 4, tt * 128:(tt + 1) * 128],
                         i=self.ps_ap(4, 0, 512).rearrange("p (c t) -> p c t", t=128):
                         h.scalar_tensor_tensor(out=o, in0=i, scalar=1.0, in1=o,
                                                op0=ALU.mult, op1=ALU.add),
                         reads=[self.pbuf(4), b_S], writes=[b_S], tag=tag + ".fad")
        sce.close()
        scf.close()

    def finish(self, block, qs, ms, S_, b_S, o_y, b_y):
        mk = self.mk
        mk.dma("sp", lambda h: h.dma_start(
            out=o_y.rearrange("(c p) t -> p c t", p=128),
            in_=S_[:].rearrange("p (c t) -> p c t", c=KC)[:, :, 0:TOWN]),
            reads=[b_S], writes=[b_y], tag="st_y")
        mk.op("pool", lambda h: h.nop(), reads=[b_y], tag="final")
        mk.finalize(block, qs, ms)
        print("kernel build stats:", mk.stats(), flush=True)
        self.st.close()
        return self.nc


def _slabify(W):
    IN, OUT = W.shape
    ic, oc = IN // 128, OUT // 128
    return np.ascontiguousarray(
        W.reshape(ic, 128, oc, 128).transpose(2, 1, 0, 3).reshape(oc, 128, ic * 128))


def _sigma_perm():
    return np.concatenate([np.arange(0, 64, 2), np.arange(1, 64, 2)])


def _rope_tables(pos, scale):
    half = 32
    inv = 1.0 / (10000.0 ** (np.arange(half, dtype=np.float64) * 2.0 / 64.0))
    ang = pos[:, None].astype(np.float64) * inv[None, :]
    c = np.cos(ang).T.astype(np.float32) * scale
    s = np.sin(ang).T.astype(np.float32) * scale
    return (np.ascontiguousarray(np.vstack([c, c])),
            np.ascontiguousarray(np.vstack([s, s])))


def _make_bmask(hh):
    WINDOW, PERSIST = 128, 16
    out = np.zeros((128, 8 * 384), np.float32)
    for qt in range(8):
        qpos = hh * 1024 + qt * 128 + np.arange(128)
        haloA = np.arange(128)
        prev = (896 + np.arange(128)) if qt == 0 else hh * 1024 + (qt - 1) * 128 + np.arange(128)
        own = hh * 1024 + qt * 128 + np.arange(128)
        slots = [haloA, prev, own]
        valid = []
        for kp in slots:
            q = qpos[:, None]
            k = kp[None, :]
            valid.append((k <= q) & ((q - k < WINDOW) | (k < PERSIST)))
        seen = {}
        for sl in (2, 1, 0):
            kp = slots[sl]
            for j, p in enumerate(kp):
                if p in seen:
                    valid[sl][:, j] = False
                else:
                    seen[p] = sl
        for sl in range(3):
            out[:, qt * 384 + sl * 128: qt * 384 + (sl + 1) * 128] = np.where(
                valid[sl], 0.0, NEG)
    return out


_NC_CACHE = {}


def _get_nc():
    key = NPHASE
    if key not in _NC_CACHE:
        _NC_CACHE[key] = B().build()
    return _NC_CACHE[key]


def kernel(**inputs):
    x = np.asarray(inputs["L0_attn_a_hidden_states"])
    sig = _sigma_perm()

    Wqb = np.asarray(inputs["L0_attn_a_Wqb"])
    perm = np.concatenate([np.concatenate([h * 128 + np.arange(64),
                                           h * 128 + 64 + sig]) for h in range(H)])
    wqb = _slabify(np.ascontiguousarray(Wqb[:, perm]))

    Wkva = np.asarray(inputs["L0_attn_a_Wkva"])
    wkva_c = _slabify(np.ascontiguousarray(Wkva[:, :512]))
    Wkp = np.ascontiguousarray(Wkva[:, 512 + sig])
    wkva_p = np.ascontiguousarray(
        Wkp.reshape(KC, 128, 64).transpose(1, 0, 2).reshape(128, KC * 64))

    Wkvb = np.asarray(inputs["L0_attn_a_Wkvb"])
    nope_cols = []
    v_cols = []
    for h in range(H):
        nope_cols.append(h * 192 + np.arange(64))
        v_cols.append(h * 192 + 64 + np.arange(128))
    wkvb_n = np.stack([
        _slabify(np.ascontiguousarray(
            Wkvb[:, np.concatenate([nope_cols[2 * p], nope_cols[2 * p + 1]])]))[0]
        for p in range(8)])
    wkvb_v = np.stack([
        np.ascontiguousarray(
            Wkvb[:, np.concatenate([v_cols[2 * p], v_cols[2 * p + 1]])]
            .reshape(KVLC, 128, 256))
        for p in range(8)])

    woa = np.ascontiguousarray(np.asarray(inputs["L0_attn_a_Wo"]).reshape(KC, 128, D))
    wqa = _slabify(np.asarray(inputs["L0_attn_a_Wqa"]))
    wqbb = _slabify(np.asarray(inputs["L0_attn_b_Wq"]))
    wkvb2 = _slabify(np.asarray(inputs["L0_attn_b_Wkv"]))
    wkb = _slabify(np.asarray(inputs["L0_attn_b_Wk"]))
    Wvb = np.asarray(inputs["L0_attn_b_Wv"])
    wvb = np.stack([np.ascontiguousarray(
        Wvb[:, g * 512:(g + 1) * 512].reshape(KVLC, 128, 512)) for g in range(4)])
    wob = np.ascontiguousarray(np.asarray(inputs["L0_attn_b_Wo"]).reshape(KC, 128, D))

    WrWsg = np.concatenate([np.asarray(inputs["L0_ffn_Wr"]),
                            np.asarray(inputs["L0_ffn_Wsg"])], axis=1)
    wrsg = np.ascontiguousarray(
        WrWsg.reshape(KC, 128, 9).transpose(1, 0, 2).reshape(128, KC * 9))

    Wg = np.asarray(inputs["L0_ffn_Wg"])
    Wu = np.asarray(inputs["L0_ffn_Wu"])
    Wd = np.asarray(inputs["L0_ffn_Wd"])
    wg = np.stack([_slabify(Wg[e]) for e in range(NE)])
    wu = np.stack([_slabify(Wu[e]) for e in range(NE)])
    wd = np.stack([_slabify(Wd[e]) for e in range(NE)])
    sgw = _slabify(np.asarray(inputs["L0_ffn_Sg"]))
    suw = _slabify(np.asarray(inputs["L0_ffn_Su"]))
    Sd = np.asarray(inputs["L0_ffn_Sd"])
    sdw_full = _slabify(Sd)
    sdw = np.ascontiguousarray(
        sdw_full.reshape(KC, 128, 4, FE).transpose(0, 2, 1, 3))

    causal = np.where(np.arange(128)[None, :] <= np.arange(128)[:, None],
                      0.0, NEG).astype(np.float32)
    ident = np.eye(128, dtype=np.float32)
    triu = np.triu(np.ones((128, 128), np.float32), k=1)
    jj8, ii8 = np.meshgrid(np.arange(80), np.arange(80), indexing="ij")
    tri8 = (((jj8 % 8) == (ii8 % 8)) & (jj8 < ii8)).astype(np.float32)
    tokid = (np.arange(10)[None, :] * 128 + np.arange(128)[:, None]).astype(np.int32)
    cosk, sink = _rope_tables(np.arange(S), C1)

    shared = dict(wqa=wqa, wqb=wqb, wkva_c=wkva_c, wkva_p=wkva_p, wkvb_n=wkvb_n,
                  wkvb_v=wkvb_v, woa=woa, wqbb=wqbb, wkvb2=wkvb2, wkb=wkb,
                  wvb=wvb, wob=wob, wrsg=wrsg, wg=wg, wu=wu, wd=wd,
                  sgw=sgw, suw=suw, sdw=sdw, causal=causal, ident=ident,
                  cosk=cosk, sink=sink, triu=triu, tri8=tri8, tokid=tokid)

    in_maps = []
    for core in range(8):
        b, hh = core // 2, core % 2
        xT = np.ascontiguousarray(x[b].T)
        qpos = np.concatenate([hh * 1024 + np.arange(1024),
                               np.arange(128), 896 + np.arange(128)])
        xq = np.ascontiguousarray(xT[:, qpos])
        cosq, sinq = _rope_tables(qpos, C1)
        m = dict(shared)
        m["xq"] = xq
        m["xkv"] = xT
        m["dmA"] = causal if hh == 0 else np.zeros((128, 128), np.float32)
        m["hflag"] = np.full((1, 128), 1.0 if hh == 0 else 0.0, np.float32)
        m["cosq"] = cosq
        m["sinq"] = sinq
        m["bmask"] = _make_bmask(hh)
        in_maps.append(m)

    nc = _get_nc()
    res = run_bass_kernel_spmd(nc, in_maps, core_ids=list(range(8)))
    out = np.empty((4, 2048, 2048), np.float32)
    for core in range(8):
        b, hh = core // 2, core % 2
        out[b, hh * 1024:(hh + 1) * 1024, :] = res.results[core]["y"].T
    return out

